# version 06
# baseline (speedup 1.0000x reference)
import sys
import numpy as np
import ml_dtypes

for _p in ("/opt/trn_rl_repo",):
    if _p not in sys.path:
        sys.path.insert(0, _p)

import concourse.bass as bass
import concourse.mybir as mybir
from concourse.bass_utils import run_bass_kernel_spmd
from contextlib import ExitStack

F32 = mybir.dt.float32
BF16 = mybir.dt.bfloat16
AF = mybir.ActivationFunctionType
ALU = mybir.AluOpType
AX = mybir.AxisListType

P = 128
D = 1024
H, C = 16, 64
HH = 8
HID = 1024
E = 8
F2 = 2048
F = 1024
NCHUNK = 4
QC = C // NCHUNK
ALPHA = 1.702
INV_ALPHA = 1.0 / ALPHA
LIMIT = 7.0
EPS = 1e-5
NEGBIG = -1.0e30

ENGS = ("tensor", "scalar", "vector", "sync", "gpsimd")


class Sched:

    def __init__(self):
        self.ops = []
        self.byname = {}
        self._last_dma = {}

    def op(self, engine, emit, after=(), name=None, dma=False):
        i = len(self.ops)
        after = tuple(a for a in after if a)
        if dma:
            prev = self._last_dma.get(engine)
            if prev is not None and prev not in after:
                after = after + (prev,)
            assert name, "dma ops must be named"
            self._last_dma[engine] = name
        self.ops.append(dict(engine=engine, emit=emit, after=after,
                             name=name, dma=dma, idx=i))
        if name:
            assert name not in self.byname, f"dup event {name}"
            self.byname[name] = i
        return name

    def alias(self, new, old):
        self.byname[new] = self.byname[old]

    def build(self, block, sems):
        need_sig = set(o["idx"] for o in self.ops if o["dma"])
        for o in self.ops:
            for a in o["after"]:
                assert a in self.byname, f"unknown event {a!r} needed by {o['name'] or o['idx']}"
                j = self.byname[a]
                assert j < o["idx"], f"event {a} not before {o['name'] or o['idx']}"
                if self.ops[j]["engine"] != o["engine"] or self.ops[j]["dma"]:
                    need_sig.add(j)
        counters = {e: 0 for e in ENGS}
        sigval = {}
        for o in self.ops:
            if o["idx"] in need_sig:
                counters[o["engine"]] += 16 if o["dma"] else 1
                sigval[o["idx"]] = counters[o["engine"]]
        self.final_counts = dict(counters)
        for eng_name in ENGS:
            eng_ops = [o for o in self.ops if o["engine"] == eng_name]
            if not eng_ops:
                continue

            def mk(eng_ops=eng_ops, eng_name=eng_name):
                def body(eng):
                    hw = {e: 0 for e in ENGS}
                    for o in eng_ops:
                        for a in o["after"]:
                            j = self.byname[a]
                            pe = self.ops[j]["engine"]
                            if pe == eng_name and not self.ops[j]["dma"]:
                                continue
                            v = sigval[j]
                            if v > hw[pe]:
                                eng.wait_ge(sems[pe], v)
                                hw[pe] = v
                        inst = o["emit"](eng)
                        if o["idx"] in need_sig:
                            assert inst is not None, f"op {o['name']} returned no instruction"
                            inst.then_inc(sems[o["engine"]], 16 if o["dma"] else 1)
                return body
            getattr(block, eng_name)(mk())


def build_nc(TPC=4096, TG=1024, max_ops=None, taps=False):
    NT = TG // P
    NG = TPC // TG
    assert NT * P == TG and NG * TG == TPC
    TS = min(512, TG)

    nc = bass.Bass("TRN2", target_bir_lowering=False, debug=False,
                   detect_race_conditions=False)
    x_in = nc.dram_tensor("x_in", [TPC, D], F32, kind="ExternalInput").ap()
    wq_in = nc.dram_tensor("wq_in", [D, HID], F32, kind="ExternalInput").ap()
    wk_in = nc.dram_tensor("wk_in", [D, HID], F32, kind="ExternalInput").ap()
    wv_in = nc.dram_tensor("wv_in", [D, HID], F32, kind="ExternalInput").ap()
    wo_in = nc.dram_tensor("wo_in", [HID, D], F32, kind="ExternalInput").ap()
    brow_in = nc.dram_tensor("brow_in", [1, 4 * 1024], F32, kind="ExternalInput").ap()
    b2s_in = nc.dram_tensor("b2s_in", [E + 1, D], F32, kind="ExternalInput").ap()
    bcol_in = nc.dram_tensor("bcol_in", [P, 288], F32, kind="ExternalInput").ap()
    gwT_in = nc.dram_tensor("gwT_in", [P, 64], F32, kind="ExternalInput").ap()
    gb_in = nc.dram_tensor("gb_in", [P, E], F32, kind="ExternalInput").ap()
    w1_in = nc.dram_tensor("w1_in", [E, D, F2], F32, kind="ExternalInput").ap()
    w3_in = nc.dram_tensor("w3_in", [E, D, F2], F32, kind="ExternalInput").ap()
    w2_in = nc.dram_tensor("w2_in", [E, F, D], F32, kind="ExternalInput").ap()
    sw1_in = nc.dram_tensor("sw1_in", [D, F2], F32, kind="ExternalInput").ap()
    sw3_in = nc.dram_tensor("sw3_in", [D, F2], F32, kind="ExternalInput").ap()
    sw2_in = nc.dram_tensor("sw2_in", [F, D], F32, kind="ExternalInput").ap()
    eye_in = nc.dram_tensor("eye_in", [P, P], F32, kind="ExternalInput").ap()
    msk_in = nc.dram_tensor("msk_in", [P, QC * QC], BF16, kind="ExternalInput").ap()
    out_t = nc.dram_tensor("out", [TPC, D], F32, kind="ExternalOutput").ap()
    tap_t = {}
    if taps:
        for nm, cols, dt in [("tap_kb", 1024, BF16), ("tap_vb", 1024, BF16),
                             ("tap_dinv", 1024, F32), ("tap_attn", 1024, F32),
                             ("tap_r1", 1024, F32), ("tap_ln1", 1024, F32),
                             ("tap_gsc", E, F32), ("tap_cw", E, F32),
                             ("tap_acc", 1024, F32),
                             ("tap_ln1fm", 8 * TG, BF16), ("tap_afm", 8 * TG, BF16)]:
            tap_t[nm] = nc.dram_tensor(nm, [P, cols], dt, kind="ExternalOutput").ap()

    es = ExitStack()
    sb = lambda name, shape, dt: es.enter_context(nc.sbuf_tensor(name, shape, dt))

    with es:
        eye = sb("eye", [P, P], F32)
        msk = sb("msk", [P, QC * QC], BF16)
        gwT = sb("gwT", [P, 64], BF16)
        gbr = sb("gbr", [P, E], F32)
        brow = sb("brow", [1, 4 * 1024], BF16)
        b2row = sb("b2row", [1, D], BF16)
        bcol = sb("bcol", [P, 288], F32)
        ones_r = sb("ones_r", [1, P], BF16)

        wbig = sb("wbig", [P, 4 * 8192], BF16)
        wqs = wbig[:, 0:8192]
        wks = wbig[:, 8192:16384]
        wvs = wbig[:, 16384:24576]
        wos = wbig[:, 24576:32768]
        w1g = wbig[:, 0:2048]
        w1l = wbig[:, 2048:4096]
        w3g = wbig[:, 4096:6144]
        w3l = wbig[:, 6144:8192]
        w2s = wbig[:, 8192:16384]

        xf = sb("xf", [P, D], F32)
        kb = sb("kb", [P, HID], BF16)
        vb = sb("vb", [P, HID], BF16)
        se0 = sb("se0", [P, HH * QC * C], BF16)
        se1 = sb("se1", [P, HH * QC * C // 2], BF16)
        se = [se0, se1]
        nbuf = sb("nbuf", [P, HID], F32)
        dbuf = sb("dbuf", [P, HID], F32)
        attnfm = sb("attnfm", [P, HID], BF16)
        xfm = attnfm
        r1 = sb("r1", [P, D], F32)
        sq = sb("sqs", [P, D], F32)
        assert TG <= D
        silu = sq
        t3b = sb("t3b", [P, TG], F32)
        ug = sb("ug", [P, TG], F32)
        ul = sb("ul", [P, TG], F32)
        ln1fm = sb("ln1fm", [P, 8 * TG], BF16)
        afm = sb("afm", [P, 8 * TG], BF16)
        acc = sb("acc", [P, NT * D], F32)
        cw = sb("cw", [P, NT * E], F32)
        gsc = sb("gsc", [P, E], F32)
        gtmp = sb("gtmp", [P, E], F32)
        gtmp2 = sb("gtmp2", [P, E], F32)
        colA = sb("colA", [P, 1], F32)
        colB = sb("colB", [P, 1], F32)
        colC = sb("colC", [P, 1], F32)
        colD = sb("colD", [P, 1], F32)

        outb = r1

        ps = [es.enter_context(nc.psum_tensor(f"ps{i}", [P, 1024], F32)) for i in range(4)]
        pgate = ps[2][:, 512:512 + E]

        sems = {e: es.enter_context(nc.semaphore(f"sem_{e}")) for e in ENGS}
        sch = Sched()
        block = es.enter_context(nc.Block())

        def ld(engq, dst, src, name, after=()):
            sch.op(engq, lambda eng, dst=dst, src=src: eng.dma_start(out=dst, in_=src),
                   after=after, name=name, dma=True)

        def tapst(nm, src, after):
            if taps and nm not in sch.byname:
                sch.op("sync", lambda s, src=src: s.dma_start(out=tap_t[nm], in_=src),
                       after=after, name=nm, dma=True)
                return nm
            return None

        ld("sync", bcol[:], bcol_in, "c_bcol")
        ld("sync", gbr[:], gb_in, "c_gb")
        ld("sync", eye[:], eye_in, "c_eye")
        ld("gpsimd", msk[:], msk_in, "c_msk")
        ld("gpsimd", gwT[:], gwT_in, "c_gw")
        ld("gpsimd", brow[:], brow_in, "c_brow")
        sch.op("vector", lambda v: v.memset(ones_r[:], 1.0), name="c_ones")

        def brow_s(i, js):
            return brow[0:1, i * 1024 + js * 512: i * 1024 + (js + 1) * 512]

        def w_src_qkv(w):
            return w.rearrange("(kc p) n -> p kc n", kc=8, p=P)

        def r3(ap, kc=8):
            return ap.rearrange("p (kc n) -> p kc n", kc=kc)

        def emit_group(g):
            pg = f"g{g}"
            aft_arena = (f"g{g-1}_ffn_mm_done",) if g else ("c_brow",)
            ld("gpsimd", r3(wqs), w_src_qkv(wq_in), f"{pg}_lwq", after=aft_arena)
            ld("gpsimd", r3(wks), w_src_qkv(wk_in), f"{pg}_lwk", after=aft_arena)
            ld("gpsimd", r3(wvs), w_src_qkv(wv_in), f"{pg}_lwv", after=aft_arena)
            ld("gpsimd", r3(wos), w_src_qkv(wo_in), f"{pg}_lwo",
               after=(f"g{g-1}_ao_done",) if g else ())

            for i in range(NT):
                t = f"{pg}t{i}"
                rows = slice((g * NT + i) * P, (g * NT + i + 1) * P)
                aft_xf = ()
                if i >= 1:
                    aft_xf = (f"{pg}t{i-1}_r1",)
                elif g > 0:
                    aft_xf = (f"g{g-1}t{NT-1}_r1",)
                ld("sync", xf[:], x_in[rows, :], f"{t}_lx", after=aft_xf)

                def tx(eng):
                    for c2 in range(8):
                        inst = eng.transpose(ps[3][:, c2 * P:(c2 + 1) * P],
                                             xf[:, c2 * P:(c2 + 1) * P], eye[:])
                    return inst
                aft = [f"{t}_lx", "c_eye"]
                if i > 0:
                    aft.append(f"{pg}t{i-1}_elnfm")
                elif g > 0:
                    aft.append(f"g{g-1}_ffn_done")
                sch.op("tensor", tx, after=aft, name=f"{t}_tx")
                sch.op("scalar", lambda s: s.copy(xfm[:], ps[3][:]),
                       after=(f"{t}_tx",), name=f"{t}_exfm")

                def qkv(eng):
                    for pi in range(3):
                        for js in range(2):
                            eng.matmul(ps[pi][:, js * 512:(js + 1) * 512], ones_r[0:1, :],
                                       brow_s(pi, js), start=True, stop=False)
                    for kc in range(8):
                        for pi, w in ((0, wqs), (1, wks), (2, wvs)):
                            for js in range(2):
                                inst = eng.matmul(
                                    ps[pi][:, js * 512:(js + 1) * 512],
                                    xfm[:, kc * P:(kc + 1) * P],
                                    w[:, kc * HID + js * 512: kc * HID + (js + 1) * 512],
                                    start=False, stop=(kc == 7))
                    return inst
                aft = [f"{t}_exfm", f"{pg}_lwq", f"{pg}_lwk", f"{pg}_lwv", "c_ones", "c_brow"]
                if i > 0:
                    aft += [f"{pg}t{i-1}_r1", f"{pg}t{i-1}_eqkv", f"{pg}t{i-1}_attnb",
                            f"{pg}t{i-1}_egate"]
                elif g > 0:
                    aft += [f"g{g-1}t{NT-1}_r1", f"g{g-1}t{NT-1}_eqkv",
                            f"g{g-1}t{NT-1}_attnb", f"g{g-1}t{NT-1}_egate"]
                sch.op("tensor", qkv, after=aft, name=f"{t}_qkv")

                def eqkv(s):
                    s.copy(kb[:], ps[1][:])
                    return s.copy(vb[:], ps[2][:])
                aft = [f"{t}_qkv"]
                if i:
                    aft.append(f"{pg}t{i-1}_attnb")
                elif g:
                    aft.append(f"g{g-1}t{NT-1}_attnb")
                sch.op("scalar", eqkv, after=aft, name=f"{t}_eqkv")
                if g == 0 and i == 0:
                    tapst("tap_kb", kb[:], (f"{t}_eqkv",))
                    tapst("tap_vb", vb[:], (f"{t}_eqkv",))

                CH_ORDER = [(2, 0, 0), (0, 0, 1), (2, 1, 0), (0, 1, 1),
                            (3, 0, 0), (1, 0, 1), (3, 1, 0), (1, 1, 1)]
                for n_, (c, hh, bsel) in enumerate(CH_ORDER):
                    kext = QC * (c + 1)
                    sebuf = se[bsel]
                    s4 = sebuf[:, :HH * QC * kext].rearrange(
                        "p (h q k) -> p h q k", h=HH, q=QC, k=kext)
                    hsl = slice(hh * HH, (hh + 1) * HH)
                    q_ap = ps[0][:, :HID].rearrange("p (h q) -> p h q", h=H)[:, hsl, c * QC:(c + 1) * QC] \
                        .unsqueeze(3).broadcast_to([P, HH, QC, kext])
                    k_ap = kb[:].rearrange("p (h k) -> p h k", h=H)[:, hsl, 0:kext] \
                        .unsqueeze(2).broadcast_to([P, HH, QC, kext])
                    v_ap = vb[:].rearrange("p (h k) -> p h k", h=H)[:, hsl, 0:kext] \
                        .unsqueeze(2).broadcast_to([P, HH, QC, kext])
                    aft = [f"{t}_eqkv", f"{t}_qkv"]
                    if n_ >= 2:
                        pc, ph, _ = CH_ORDER[n_ - 2]
                        aft.append(f"{t}_exp{pc}{ph}")
                    sch.op("vector",
                           lambda v, s4=s4, q_ap=q_ap, k_ap=k_ap:
                           v.tensor_tensor(s4, q_ap, k_ap, op=ALU.mult),
                           after=aft, name=f"{t}_s{c}{hh}")
                    sch.op("scalar",
                           lambda s, s4=s4: s.activation(s4, s4, AF.Exp, scale=0.125),
                           after=(f"{t}_s{c}{hh}",), name=f"{t}_exp{c}{hh}")
                    diag = s4[:, :, :, c * QC:(c + 1) * QC]
                    m_ap = msk[:].rearrange("p (q k) -> p q k", q=QC, k=QC) \
                        .unsqueeze(1).broadcast_to([P, HH, QC, QC])
                    sch.op("vector",
                           lambda v, diag=diag, m_ap=m_ap:
                           v.tensor_tensor(diag, diag, m_ap, op=ALU.mult),
                           after=(f"{t}_exp{c}{hh}", "c_msk"), name=f"{t}_msk{c}{hh}")
                    d_out = dbuf[:].rearrange("p (h q) -> p h q", h=H)[:, hsl, c * QC:(c + 1) * QC]
                    sch.op("vector",
                           lambda v, d_out=d_out, s4=s4:
                           v.tensor_reduce(d_out, s4, axis=AX.X, op=ALU.add),
                           name=f"{t}_dred{c}{hh}")
                    sch.op("vector",
                           lambda v, s4=s4, v_ap=v_ap:
                           v.tensor_tensor(s4, s4, v_ap, op=ALU.mult),
                           name=f"{t}_ev{c}{hh}")
                    n_out = nbuf[:].rearrange("p (h q) -> p h q", h=H)[:, hsl, c * QC:(c + 1) * QC]
                    sch.op("vector",
                           lambda v, n_out=n_out, s4=s4:
                           v.tensor_reduce(n_out, s4, axis=AX.X, op=ALU.add),
                           name=f"{t}_nred{c}{hh}")

                sch.op("vector", lambda v: v.reciprocal(dbuf[:], dbuf[:]),
                       name=f"{t}_dinv")
                sch.op("vector", lambda v: v.tensor_tensor(nbuf[:], nbuf[:], dbuf[:], op=ALU.mult),
                       name=f"{t}_attnb")
                if g == 0 and i == 0:
                    tapst("tap_dinv", dbuf[:], (f"{t}_attnb",))
                    tapst("tap_attn", nbuf[:], (f"{t}_attnb",))

                def tattn(eng):
                    for c2 in range(8):
                        inst = eng.transpose(ps[3][:, c2 * P:(c2 + 1) * P],
                                             nbuf[:, c2 * P:(c2 + 1) * P], eye[:])
                    return inst
                sch.op("tensor", tattn, after=(f"{t}_attnb", f"{t}_exfm"), name=f"{t}_tattn")
                sch.op("scalar", lambda s: s.copy(attnfm[:], ps[3][:]),
                       after=(f"{t}_tattn",), name=f"{t}_eattnfm")

                def attno(eng):
                    for js in range(2):
                        eng.matmul(ps[0][:, js * 512:(js + 1) * 512], ones_r[0:1, :],
                                   brow_s(3, js), start=True, stop=False)
                    for kc in range(8):
                        for js in range(2):
                            inst = eng.matmul(
                                ps[0][:, js * 512:(js + 1) * 512],
                                attnfm[:, kc * P:(kc + 1) * P],
                                wos[:, kc * D + js * 512: kc * D + (js + 1) * 512],
                                start=False, stop=(kc == 7))
                    return inst
                sch.op("tensor", attno, after=(f"{t}_eattnfm", f"{pg}_lwo"), name=f"{t}_ao")
                if i == NT - 1:
                    sch.alias(f"{pg}_ao_done", f"{t}_ao")

                aft_r1 = [f"{t}_ao", f"{t}_lx"]
                if i == 0 and g > 0:
                    aft_r1.append(f"g{g-1}o{NT-1}_st")
                sch.op("vector", lambda v: v.tensor_tensor(r1[:], ps[0][:, :D], xf[:], op=ALU.add),
                       after=aft_r1, name=f"{t}_r1")
                if g == 0 and i == 0:
                    tapst("tap_r1", r1[:], (f"{t}_r1",))

                sch.op("vector", lambda v: v.tensor_reduce(colA[:], r1[:], axis=AX.X, op=ALU.add),
                       name=f"{t}_s1")
                sch.op("scalar", lambda s: s.activation(sq[:], r1[:], AF.Square),
                       after=(f"{t}_r1",), name=f"{t}_sq")
                def ln_small(v):
                    v.tensor_reduce(colB[:], sq[:], axis=AX.X, op=ALU.add)
                    v.drain()
                    v.tensor_scalar_mul(colA[:], colA[:], 1.0 / D)
                    v.tensor_scalar_mul(colB[:], colB[:], 1.0 / D)
                    v.drain()
                    v.tensor_tensor(colC[:], colA[:], colA[:], op=ALU.mult)
                    v.drain()
                    v.tensor_tensor(colB[:], colB[:], colC[:], op=ALU.subtract)
                    v.drain()
                    return v.tensor_scalar_add(colB[:], colB[:], float(EPS))
                sch.op("vector", ln_small, after=(f"{t}_sq",), name=f"{t}_var")
                sch.op("scalar", lambda s: s.activation(colC[:], colB[:], AF.Sqrt),
                       after=(f"{t}_var",), name=f"{t}_sqrt")
                sch.op("vector", lambda v: v.reciprocal(colD[:], colC[:]),
                       after=(f"{t}_sqrt",), name=f"{t}_rstd")
                accsl = acc[:, i * D:(i + 1) * D]
                aft_ln1 = []
                if g > 0:
                    aft_ln1.append(f"g{g-1}o{i}_ln2")
                def ln1n(v, accsl=accsl):
                    v.tensor_scalar(accsl, r1[:], colA[:], None, op0=ALU.subtract)
                    return v.tensor_scalar(accsl, accsl, colD[:], None, op0=ALU.mult)
                sch.op("vector", ln1n, after=aft_ln1, name=f"{t}_ln1")
                if g == 0 and i == 0:
                    tapst("tap_ln1", acc[:, 0:D], (f"{t}_ln1",))

                def tln(eng, accsl=accsl):
                    for c2 in range(8):
                        inst = eng.transpose(ps[3][:, c2 * P:(c2 + 1) * P],
                                             accsl[:, c2 * P:(c2 + 1) * P], eye[:])
                    return inst
                sch.op("tensor", tln, after=(f"{t}_ln1", f"{t}_eattnfm"), name=f"{t}_tln")
                lnout = ln1fm[:].rearrange("p (dc t) -> p dc t", dc=8)[:, :, i * P:(i + 1) * P]
                aft_elnfm = [f"{t}_tln"]
                if i == 0 and g > 0:
                    aft_elnfm.append(f"g{g-1}_ffn_mm_done")
                sch.op("scalar", lambda s, lnout=lnout: s.copy(
                    lnout, ps[3][:].rearrange("p (dc t) -> p dc t", dc=8, t=P)),
                    after=tuple(aft_elnfm), name=f"{t}_elnfm")

                def gate(eng, i=i):
                    for kc in range(8):
                        inst = eng.matmul(
                            pgate,
                            ln1fm[:, kc * TG + i * P: kc * TG + (i + 1) * P],
                            gwT[:, kc * E:(kc + 1) * E],
                            start=(kc == 0), stop=(kc == 7))
                    return inst
                aft_gate = [f"{t}_elnfm", "c_gw"]
                if i:
                    aft_gate.append(f"{pg}t{i-1}_cw")
                elif g:
                    aft_gate.append(f"g{g-1}t{NT-1}_cw")
                sch.op("tensor", gate, after=aft_gate, name=f"{t}_gate")
                sch.op("scalar", lambda s: s.copy(gsc[:], pgate),
                       after=(f"{t}_gate",), name=f"{t}_egate")
                sch.op("scalar", lambda s: s.activation(gtmp[:], gsc[:], AF.Exp),
                       after=(f"{t}_egate",), name=f"{t}_gexp")
                cwsl = cw[:, i * E:(i + 1) * E]
                def top2(v, cwsl=cwsl):
                    colAb = colA[:].broadcast_to([P, E])
                    v.tensor_reduce(colB[:], gtmp[:], axis=AX.X, op=ALU.add)
                    v.drain()
                    v.reciprocal(colC[:], colB[:])
                    v.drain()
                    v.tensor_scalar(gsc[:], gtmp[:], colC[:], None, op0=ALU.mult)
                    v.drain()
                    v.tensor_tensor(gtmp[:], gsc[:], gbr[:], op=ALU.add)
                    v.drain()
                    v.reduce_max(colA[:], gtmp[:], axis=AX.X)
                    v.drain()
                    v.tensor_tensor(gtmp2[:], gtmp[:], colAb, op=ALU.is_equal)
                    v.drain()
                    v.scalar_tensor_tensor(gtmp[:], gtmp2[:], NEGBIG, gtmp[:], op0=ALU.mult, op1=ALU.add)
                    v.drain()
                    v.reduce_max(colA[:], gtmp[:], axis=AX.X)
                    v.drain()
                    v.tensor_tensor(gtmp[:], gtmp[:], colAb, op=ALU.is_equal)
                    v.drain()
                    v.tensor_tensor(gtmp[:], gtmp[:], gtmp2[:], op=ALU.add)
                    v.drain()
                    return v.tensor_tensor(cwsl, gsc[:], gtmp[:], op=ALU.mult)
                sch.op("vector", top2, after=(f"{t}_gexp", "c_gb"), name=f"{t}_cw")
                if g == 0 and i == 0:
                    tapst("tap_gsc", gsc[:], (f"{t}_cw",))
                    tapst("tap_cw", cw[:, 0:E], (f"{t}_cw",))

                if g == 0 and i == NT - 1:
                    tapst("tap_ln1fm", ln1fm[:], (f"{t}_elnfm",))

            def w13_src(base, pair):
                return base.rearrange("(kc p) f -> p kc f", kc=8, p=P)[:, :, pair * 256:(pair + 1) * 256]

            NTS = TG // TS
            for e_ in range(E + 1):
                x_ = f"{pg}e{e_}"
                w1b = w1_in[e_] if e_ < E else sw1_in
                w3b = w3_in[e_] if e_ < E else sw3_in
                w2b = (w2_in[e_] if e_ < E else sw2_in).rearrange("(fc p) d2 -> p fc d2", fc=8, p=P)

                aft = [f"{pg}_w2mm_{e_-1}_{NT-1}"] if e_ else [f"{pg}t{NT-1}_qkv"]
                ld("gpsimd", r3(w2s), w2b, f"{x_}_lw2", after=aft)
                ld("gpsimd", b2row[:], b2s_in[e_:e_ + 1, :], f"{x_}_lb2", after=aft)
                sch.op("vector", lambda v: v.tensor_scalar_mul(w2s, w2s, INV_ALPHA),
                       after=(f"{x_}_lw2",), name=f"{x_}_w2sc")

                if e_ < E:
                    b1o, b3o = 16 * e_, 128 + 16 * e_
                else:
                    b1o, b3o = 256, 272

                for m in range(4):
                    def hq(ee, mm):
                        return [f"{pg}e{ee}_h1g_{2*mm+1}", f"{pg}e{ee}_h3g_{2*mm+1}",
                                f"{pg}e{ee}_h1l_{2*mm+1}", f"{pg}e{ee}_h3l_{2*mm+1}"]
                    if m > 0:
                        aft_l = hq(e_, m - 1)
                    elif e_ > 0:
                        aft_l = hq(e_ - 1, 3)
                    else:
                        aft_l = [f"{pg}t{NT-1}_qkv"]
                    ld("gpsimd", r3(w1g, 8), w13_src(w1b, m), f"{x_}_lw1g_{m}", after=aft_l)
                    ld("gpsimd", r3(w1l, 8), w13_src(w1b, 4 + m), f"{x_}_lw1l_{m}", after=aft_l)
                    ld("gpsimd", r3(w3g, 8), w13_src(w3b, m), f"{x_}_lw3g_{m}", after=aft_l)
                    ld("gpsimd", r3(w3l, 8), w13_src(w3b, 4 + m), f"{x_}_lw3l_{m}", after=aft_l)

                    for j2 in range(2):
                        p_ = 2 * m + j2
                        fcg = p_
                        fcl = 8 + p_
                        aft_g = [f"{x_}_lw1g_{m}", f"{x_}_lw3g_{m}", f"{pg}t{NT-1}_elnfm"]
                        if p_ > 0:
                            aft_g.append(f"{x_}_ug_{p_-1}")
                        elif e_ > 0:
                            aft_g.append(f"{pg}_acc_{e_-1}_{NT-1}")
                        else:
                            aft_g += [f"{pg}t{NT-1}_r1", f"{pg}t{NT-1}_eqkv",
                                      f"{pg}t{NT-1}_attnb"]
                        def h1gm(eng, j2=j2):
                            for ts in range(NTS):
                                for kc in range(8):
                                    inst = eng.matmul(
                                        ps[0][:, ts * TS:(ts + 1) * TS],
                                        w1g[:, kc * 256 + j2 * 128: kc * 256 + (j2 + 1) * 128],
                                        ln1fm[:, kc * TG + ts * TS: kc * TG + (ts + 1) * TS],
                                        start=(kc == 0), stop=(kc == 7))
                            return inst
                        def h3gm(eng, j2=j2):
                            for ts in range(NTS):
                                for kc in range(8):
                                    inst = eng.matmul(
                                        ps[1][:, ts * TS:(ts + 1) * TS],
                                        w3g[:, kc * 256 + j2 * 128: kc * 256 + (j2 + 1) * 128],
                                        ln1fm[:, kc * TG + ts * TS: kc * TG + (ts + 1) * TS],
                                        start=(kc == 0), stop=(kc == 7))
                            return inst
                        sch.op("tensor", h1gm, after=aft_g, name=f"{x_}_h1g_{p_}")
                        sch.op("tensor", h3gm, after=aft_g, name=f"{x_}_h3g_{p_}")
                        aft_t3 = [f"{x_}_h3g_{p_}", "c_bcol"]
                        if p_ > 0:
                            aft_t3.append(f"{x_}_ul_{p_-1}")
                        elif e_ > 0:
                            aft_t3.append(f"{pg}e{e_-1}_ul_7")
                        sch.op("scalar",
                               lambda s, b3_ap=bcol[:, b3o + fcg: b3o + fcg + 1]:
                               s.activation(t3b[:, :TG], ps[1][:, :TG], AF.Identity, bias=b3_ap),
                               after=aft_t3, name=f"{x_}_t3g_{p_}")
                        aft_u = [f"{x_}_h1g_{p_}", f"{x_}_t3g_{p_}"]
                        if p_ > 0:
                            aft_u.append(f"{x_}_sil_{p_-1}")
                        elif e_ > 0:
                            aft_u.append(f"{pg}e{e_-1}_sil_7")
                        def ugf(v, b1_ap=bcol[:, b1o + fcg: b1o + fcg + 1]):
                            for c0 in range(0, TG, 512):
                                c1_ = min(c0 + 512, TG)
                                inst = v.scalar_tensor_tensor(
                                    ug[:, c0:c1_], ps[0][:, c0:c1_], b1_ap, t3b[:, c0:c1_],
                                    op0=ALU.add, op1=ALU.mult)
                            return inst
                        sch.op("vector", ugf, after=aft_u, name=f"{x_}_ug_{p_}")
                        aft_l2 = [f"{x_}_lw1l_{m}", f"{x_}_lw3l_{m}", f"{pg}t{NT-1}_elnfm"]
                        if p_ > 0:
                            aft_l2.append(f"{x_}_ul_{p_-1}")
                        elif e_ > 0:
                            aft_l2.append(f"{pg}e{e_-1}_ul_7")
                        else:
                            aft_l2 += [f"{pg}t{NT-1}_attnb", f"{pg}t{NT-1}_egate"]
                        def h1lm(eng, j2=j2):
                            for ts in range(NTS):
                                for kc in range(8):
                                    inst = eng.matmul(
                                        ps[2][:, ts * TS:(ts + 1) * TS],
                                        w1l[:, kc * 256 + j2 * 128: kc * 256 + (j2 + 1) * 128],
                                        ln1fm[:, kc * TG + ts * TS: kc * TG + (ts + 1) * TS],
                                        start=(kc == 0), stop=(kc == 7))
                            return inst
                        def h3lm(eng, j2=j2):
                            for ts in range(NTS):
                                for kc in range(8):
                                    inst = eng.matmul(
                                        ps[3][:, ts * TS:(ts + 1) * TS],
                                        w3l[:, kc * 256 + j2 * 128: kc * 256 + (j2 + 1) * 128],
                                        ln1fm[:, kc * TG + ts * TS: kc * TG + (ts + 1) * TS],
                                        start=(kc == 0), stop=(kc == 7))
                            return inst
                        sch.op("tensor", h1lm, after=aft_l2, name=f"{x_}_h1l_{p_}")
                        sch.op("tensor", h3lm, after=aft_l2, name=f"{x_}_h3l_{p_}")
                        sch.op("scalar",
                               lambda s, b3_ap=bcol[:, b3o + fcl: b3o + fcl + 1]:
                               s.activation(t3b[:, :TG], ps[3][:, :TG], AF.Identity, bias=b3_ap),
                               after=(f"{x_}_h3l_{p_}", f"{x_}_ug_{p_}"), name=f"{x_}_t3l_{p_}")
                        def ulf(v, b1_ap=bcol[:, b1o + fcl: b1o + fcl + 1]):
                            for c0 in range(0, TG, 512):
                                c1_ = min(c0 + 512, TG)
                                inst = v.scalar_tensor_tensor(
                                    ul[:, c0:c1_], ps[2][:, c0:c1_], b1_ap, t3b[:, c0:c1_],
                                    op0=ALU.add, op1=ALU.mult)
                            return inst
                        sch.op("vector", ulf, after=(f"{x_}_h1l_{p_}", f"{x_}_t3l_{p_}"),
                               name=f"{x_}_ul_{p_}")
                        sch.op("vector",
                               lambda v: v.tensor_scalar(ug[:, :TG], ug[:, :TG], LIMIT, None, op0=ALU.min),
                               name=f"{x_}_gmin_{p_}")
                        sch.op("scalar", lambda s: s.activation(silu[:, :TG], ug[:, :TG], AF.Silu, scale=ALPHA),
                               after=(f"{x_}_gmin_{p_}",), name=f"{x_}_sil_{p_}")
                        sch.op("vector",
                               lambda v: v.tensor_scalar(ul[:, :TG], ul[:, :TG], -LIMIT, LIMIT,
                                                         op0=ALU.max, op1=ALU.min),
                               name=f"{x_}_lc_{p_}")
                        if g == 0 and e_ == 1 and p_ == 0:
                            _tp = tapst("tap_afm", afm[:], (f"{pg}e0_a_7",))
                            if _tp:
                                aft_a_extra = (_tp,)
                        atgt = afm[:, fcg * TG:(fcg + 1) * TG]
                        aft_a = [f"{x_}_sil_{p_}"]
                        if e_ > 0:
                            aft_a.append(f"{pg}_w2mm_{e_-1}_{NT-1}")
                        if taps and g == 0 and e_ == 1 and p_ == 0 and "tap_afm" in sch.byname:
                            aft_a.append("tap_afm")
                        elif g > 0:
                            aft_a.append(f"g{g-1}_w2mm_{E}_{NT-1}")
                        sch.op("vector",
                               lambda v, atgt=atgt: v.scalar_tensor_tensor(
                                   atgt, ul[:, :TG], 1.0, silu[:, :TG], op0=ALU.add, op1=ALU.mult),
                               after=aft_a, name=f"{x_}_a_{p_}")


                for i in range(NT):
                    yp = ps[0] if i % 2 == 0 else ps[1]
                    def w2mm(eng, yp=yp, i=i):
                        for ds in range(2):
                            eng.matmul(yp[:, ds * 512:(ds + 1) * 512], ones_r[0:1, :],
                                       b2row[0:1, ds * 512:(ds + 1) * 512], start=True, stop=False)
                            for fc in range(8):
                                inst = eng.matmul(
                                    yp[:, ds * 512:(ds + 1) * 512],
                                    afm[:, fc * TG + i * P: fc * TG + (i + 1) * P],
                                    w2s[:, fc * D + ds * 512: fc * D + (ds + 1) * 512],
                                    start=False, stop=(fc == 7))
                        return inst
                    aft = [f"{x_}_a_7", f"{x_}_w2sc", f"{x_}_lb2"]
                    if i >= 2:
                        aft.append(f"{x_}_acc_{i-2}")
                    elif e_ == 0:
                        aft += [f"{x_}_ug_7", f"{x_}_t3g_7"]
                    else:
                        aft.append(f"{pg}_acc_{e_-1}_{NT-2+i}")
                    sch.op("tensor", w2mm, after=aft, name=f"{x_}_w2mm_{i}")
                    if i == NT - 1:
                        sch.alias(f"{pg}_w2mm_{e_}_{NT-1}", f"{x_}_w2mm_{i}")
                    accsl = acc[:, i * D:(i + 1) * D]
                    if e_ < E:
                        cwcol = cw[:, i * E + e_: i * E + e_ + 1]
                        def accf(v, yp=yp, accsl=accsl, cwcol=cwcol):
                            for c0 in range(0, D, 512):
                                v.scalar_tensor_tensor(
                                    silu[:, c0:c0 + 512], yp[:, c0:c0 + 512], cwcol,
                                    accsl[:, c0:c0 + 512], op0=ALU.mult, op1=ALU.add)
                            return v.tensor_copy(accsl, silu[:, :D])
                    else:
                        def accf(v, yp=yp, accsl=accsl):
                            return v.tensor_tensor(accsl, accsl, yp[:, :D], op=ALU.add)
                    sch.op("vector", accf, after=(f"{x_}_w2mm_{i}",), name=f"{pg}_acc_{e_}_{i}")
                    sch.alias(f"{x_}_acc_{i}", f"{pg}_acc_{e_}_{i}")
                    if g == 0 and e_ == E and i == 0:
                        tapst("tap_acc", acc[:, 0:D], (f"{pg}_acc_{e_}_{i}",))

            sch.alias(f"{pg}_ffn_mm_done", f"{pg}_w2mm_{E}_{NT-1}")
            sch.alias(f"{pg}_ffn_done", f"{pg}_acc_{E}_{NT-1}")

            for i in range(NT):
                t = f"{pg}o{i}"
                accsl = acc[:, i * D:(i + 1) * D]
                sch.op("vector", lambda v, accsl=accsl: v.tensor_reduce(colA[:], accsl, axis=AX.X, op=ALU.add),
                       after=(f"{pg}_acc_{E}_{i}",), name=f"{t}_s1")
                sch.op("scalar", lambda s, accsl=accsl: s.activation(sq[:], accsl, AF.Square),
                       after=(f"{pg}_acc_{E}_{i}",), name=f"{t}_sq")
                def ln_small2(v):
                    v.tensor_reduce(colB[:], sq[:], axis=AX.X, op=ALU.add)
                    v.drain()
                    v.tensor_scalar_mul(colA[:], colA[:], 1.0 / D)
                    v.tensor_scalar_mul(colB[:], colB[:], 1.0 / D)
                    v.drain()
                    v.tensor_tensor(colC[:], colA[:], colA[:], op=ALU.mult)
                    v.drain()
                    v.tensor_tensor(colB[:], colB[:], colC[:], op=ALU.subtract)
                    v.drain()
                    return v.tensor_scalar_add(colB[:], colB[:], float(EPS))
                sch.op("vector", ln_small2, after=(f"{t}_sq",), name=f"{t}_var")
                sch.op("scalar", lambda s: s.activation(colC[:], colB[:], AF.Sqrt),
                       after=(f"{t}_var",), name=f"{t}_sqrt")
                sch.op("vector", lambda v: v.reciprocal(colD[:], colC[:]),
                       after=(f"{t}_sqrt",), name=f"{t}_rstd")
                aft = [f"{t}_rstd"]
                if i >= 1:
                    aft.append(f"{pg}o{i-1}_st")
                elif g > 0:
                    aft.append(f"g{g-1}o{NT-1}_st")
                def ln2n(v, accsl=accsl):
                    v.tensor_scalar(outb[:], accsl, colA[:], None, op0=ALU.subtract)
                    return v.tensor_scalar(outb[:], outb[:], colD[:], None, op0=ALU.mult)
                sch.op("vector", ln2n, after=aft, name=f"{t}_ln2")
                rows = slice((g * NT + i) * P, (g * NT + i + 1) * P)
                sch.op("sync", lambda s, rows=rows: s.dma_start(out=out_t[rows, :], in_=outb[:]),
                       after=(f"{t}_ln2",), name=f"{t}_st", dma=True)

        for g in range(NG):
            emit_group(g)

        sch.op("sync", lambda s: s.nop(),
               after=tuple(f"g{NG-1}o{i}_st" for i in range(NT)), name="fin")

        if max_ops is not None:
            sch.ops = sch.ops[:max_ops]
        sch.build(block, sems)
    return nc


_NC_CACHE = {}


def _get_nc(TPC=4096, TG=1024):
    key = (TPC, TG)
    if key not in _NC_CACHE:
        _NC_CACHE[key] = build_nc(TPC, TG)
    return _NC_CACHE[key]


def prep_weights(kw):
    bf = ml_dtypes.bfloat16
    f32 = np.float32
    d = {}
    d["wq_in"] = np.ascontiguousarray(np.asarray(kw["L0_attn_Wq"], f32))
    d["wk_in"] = np.ascontiguousarray(np.asarray(kw["L0_attn_Wk"], f32))
    d["wv_in"] = np.ascontiguousarray(np.asarray(kw["L0_attn_Wv"], f32))
    d["wo_in"] = np.ascontiguousarray(np.asarray(kw["L0_attn_Wo"], f32))

    def permute_w13(w):
        wt = np.swapaxes(np.asarray(w, f32), -1, -2)
        return np.ascontiguousarray(np.concatenate([wt[..., 0::2], wt[..., 1::2]], axis=-1))

    d["w1_in"] = permute_w13(kw["L0_ffn_w1"])
    d["w3_in"] = permute_w13(kw["L0_ffn_w3"])
    d["sw1_in"] = permute_w13(kw["L0_ffn_sw1"])
    d["sw3_in"] = permute_w13(kw["L0_ffn_sw3"])
    d["w2_in"] = np.ascontiguousarray(np.swapaxes(np.asarray(kw["L0_ffn_w2"], f32), -1, -2))
    d["sw2_in"] = np.ascontiguousarray(np.asarray(kw["L0_ffn_sw2"], f32).T)

    brow = np.concatenate([
        np.asarray(kw["L0_attn_bq"], f32), np.asarray(kw["L0_attn_bk"], f32),
        np.asarray(kw["L0_attn_bv"], f32), np.asarray(kw["L0_attn_bo"], f32)])
    d["brow_in"] = np.ascontiguousarray(brow.reshape(1, 4 * 1024))
    d["b2s_in"] = np.ascontiguousarray(np.concatenate(
        [np.asarray(kw["L0_ffn_b2"], f32), np.asarray(kw["L0_ffn_sb2"], f32).reshape(1, -1)], axis=0))

    def colify(b):
        b = np.asarray(b, f32)
        bp = np.concatenate([b[0::2], b[1::2]])
        return np.ascontiguousarray(bp.reshape(16, 128).T)

    b1 = np.asarray(kw["L0_ffn_b1"], f32)
    b3 = np.asarray(kw["L0_ffn_b3"], f32)
    cols = [colify(b1[e]) for e in range(E)] + [colify(b3[e]) for e in range(E)] \
        + [colify(kw["L0_ffn_sb1"]), colify(kw["L0_ffn_sb3"])]
    d["bcol_in"] = np.ascontiguousarray(np.concatenate(cols, axis=1))

    gwT = np.asarray(kw["L0_ffn_gate_w"], f32).T.reshape(8, 128, E).transpose(1, 0, 2)
    d["gwT_in"] = np.ascontiguousarray(gwT.reshape(128, 64))
    d["gb_in"] = np.ascontiguousarray(np.broadcast_to(np.asarray(kw["L0_ffn_gate_b"], f32), (P, E)))

    d["eye_in"] = np.eye(P, dtype=f32)
    m = np.tril(np.ones((QC, QC), dtype=np.float32))
    d["msk_in"] = np.ascontiguousarray(np.broadcast_to(m.reshape(1, QC * QC), (P, QC * QC)).astype(bf))
    return d


def kernel(**inputs):
    x = np.asarray(inputs["L0_attn_x"], dtype=np.float32).reshape(-1, D)
    n_cores = 8
    shards = np.split(np.ascontiguousarray(x), n_cores, axis=0)
    wmap = prep_weights(inputs)
    nc = _get_nc(shards[0].shape[0])
    in_maps = []
    for s in shards:
        m = dict(wmap)
        m["x_in"] = s
        in_maps.append(m)
    res = run_bass_kernel_spmd(nc, in_maps, core_ids=list(range(n_cores)))
    out = np.concatenate([np.asarray(r["out"]) for r in res.results], axis=0)
    return out.reshape(2, 16384, D)


if __name__ == "__main__":
    nc = build_nc()
    print("build ok")


# version 07
# speedup vs baseline: 1.0166x; 1.0166x over previous
import sys
import numpy as np
import ml_dtypes

for _p in ("/opt/trn_rl_repo",):
    if _p not in sys.path:
        sys.path.insert(0, _p)

import concourse.bass as bass
import concourse.mybir as mybir
from concourse.bass_utils import run_bass_kernel_spmd
from contextlib import ExitStack

F32 = mybir.dt.float32
BF16 = mybir.dt.bfloat16
AF = mybir.ActivationFunctionType
ALU = mybir.AluOpType
AX = mybir.AxisListType

P = 128
D = 1024
H, C = 16, 64
HH = 8
HID = 1024
E = 8
F2 = 2048
F = 1024
NCHUNK = 4
QC = C // NCHUNK
ALPHA = 1.702
INV_ALPHA = 1.0 / ALPHA
LIMIT = 7.0
EPS = 1e-5
NEGBIG = -1.0e30

ENGS = ("tensor", "scalar", "vector", "sync", "gpsimd")


class Sched:

    def __init__(self):
        self.ops = []
        self.byname = {}
        self._last_dma = {}

    def op(self, engine, emit, after=(), name=None, dma=False):
        i = len(self.ops)
        after = tuple(a for a in after if a)
        if dma:
            prev = self._last_dma.get(engine)
            if prev is not None and prev not in after:
                after = after + (prev,)
            assert name, "dma ops must be named"
            self._last_dma[engine] = name
        self.ops.append(dict(engine=engine, emit=emit, after=after,
                             name=name, dma=dma, idx=i))
        if name:
            assert name not in self.byname, f"dup event {name}"
            self.byname[name] = i
        return name

    def alias(self, new, old):
        self.byname[new] = self.byname[old]

    def build(self, block, sems):
        need_sig = set(o["idx"] for o in self.ops if o["dma"])
        for o in self.ops:
            for a in o["after"]:
                assert a in self.byname, f"unknown event {a!r} needed by {o['name'] or o['idx']}"
                j = self.byname[a]
                assert j < o["idx"], f"event {a} not before {o['name'] or o['idx']}"
                if self.ops[j]["engine"] != o["engine"] or self.ops[j]["dma"]:
                    need_sig.add(j)
        counters = {e: 0 for e in ENGS}
        sigval = {}
        for o in self.ops:
            if o["idx"] in need_sig:
                counters[o["engine"]] += 16 if o["dma"] else 1
                sigval[o["idx"]] = counters[o["engine"]]
        self.final_counts = dict(counters)
        for eng_name in ENGS:
            eng_ops = [o for o in self.ops if o["engine"] == eng_name]
            if not eng_ops:
                continue

            def mk(eng_ops=eng_ops, eng_name=eng_name):
                def body(eng):
                    hw = {e: 0 for e in ENGS}
                    for o in eng_ops:
                        for a in o["after"]:
                            j = self.byname[a]
                            pe = self.ops[j]["engine"]
                            if pe == eng_name and not self.ops[j]["dma"]:
                                continue
                            v = sigval[j]
                            if v > hw[pe]:
                                eng.wait_ge(sems[pe], v)
                                hw[pe] = v
                        inst = o["emit"](eng)
                        if o["idx"] in need_sig:
                            assert inst is not None, f"op {o['name']} returned no instruction"
                            inst.then_inc(sems[o["engine"]], 16 if o["dma"] else 1)
                return body
            getattr(block, eng_name)(mk())


def build_nc(TPC=4096, TG=1024, max_ops=None, taps=False):
    NT = TG // P
    NG = TPC // TG
    assert NT * P == TG and NG * TG == TPC
    TS = min(512, TG)

    nc = bass.Bass("TRN2", target_bir_lowering=False, debug=False,
                   detect_race_conditions=False)
    x_in = nc.dram_tensor("x_in", [TPC, D], F32, kind="ExternalInput").ap()
    wq_in = nc.dram_tensor("wq_in", [D, HID], F32, kind="ExternalInput").ap()
    wk_in = nc.dram_tensor("wk_in", [D, HID], F32, kind="ExternalInput").ap()
    wv_in = nc.dram_tensor("wv_in", [D, HID], F32, kind="ExternalInput").ap()
    wo_in = nc.dram_tensor("wo_in", [HID, D], F32, kind="ExternalInput").ap()
    brow_in = nc.dram_tensor("brow_in", [1, 4 * 1024], F32, kind="ExternalInput").ap()
    b2s_in = nc.dram_tensor("b2s_in", [E + 1, D], F32, kind="ExternalInput").ap()
    bcol_in = nc.dram_tensor("bcol_in", [P, 288], F32, kind="ExternalInput").ap()
    gwT_in = nc.dram_tensor("gwT_in", [P, 64], F32, kind="ExternalInput").ap()
    gb_in = nc.dram_tensor("gb_in", [P, E], F32, kind="ExternalInput").ap()
    w1_in = nc.dram_tensor("w1_in", [E, D, F2], F32, kind="ExternalInput").ap()
    w3_in = nc.dram_tensor("w3_in", [E, D, F2], F32, kind="ExternalInput").ap()
    w2_in = nc.dram_tensor("w2_in", [E, F, D], F32, kind="ExternalInput").ap()
    sw1_in = nc.dram_tensor("sw1_in", [D, F2], F32, kind="ExternalInput").ap()
    sw3_in = nc.dram_tensor("sw3_in", [D, F2], F32, kind="ExternalInput").ap()
    sw2_in = nc.dram_tensor("sw2_in", [F, D], F32, kind="ExternalInput").ap()
    eye_in = nc.dram_tensor("eye_in", [P, P], F32, kind="ExternalInput").ap()
    msk_in = nc.dram_tensor("msk_in", [P, QC * QC], BF16, kind="ExternalInput").ap()
    out_t = nc.dram_tensor("out", [TPC, D], F32, kind="ExternalOutput").ap()
    tap_t = {}
    if taps:
        for nm, cols, dt in [("tap_kb", 1024, BF16), ("tap_vb", 1024, BF16),
                             ("tap_dinv", 1024, F32), ("tap_attn", 1024, F32),
                             ("tap_r1", 1024, F32), ("tap_ln1", 1024, F32),
                             ("tap_gsc", E, F32), ("tap_cw", E, F32),
                             ("tap_acc", 1024, F32),
                             ("tap_ln1fm", 8 * TG, BF16), ("tap_afm", 8 * TG, BF16)]:
            tap_t[nm] = nc.dram_tensor(nm, [P, cols], dt, kind="ExternalOutput").ap()

    es = ExitStack()
    sb = lambda name, shape, dt: es.enter_context(nc.sbuf_tensor(name, shape, dt))

    with es:
        eye = sb("eye", [P, P], F32)
        msk = sb("msk", [P, QC * QC], BF16)
        gwT = sb("gwT", [P, 64], BF16)
        gbr = sb("gbr", [P, E], F32)
        brow = sb("brow", [1, 4 * 1024], BF16)
        b2row = sb("b2row", [1, D], BF16)
        bcol = sb("bcol", [P, 288], F32)
        ones_r = sb("ones_r", [1, P], BF16)

        wbig = sb("wbig", [P, 4 * 8192], BF16)
        wqs = wbig[:, 0:8192]
        wks = wbig[:, 8192:16384]
        wvs = wbig[:, 16384:24576]
        wos = wbig[:, 24576:32768]
        w1g = wbig[:, 0:2048]
        w1l = wbig[:, 2048:4096]
        w3g = wbig[:, 4096:6144]
        w3l = wbig[:, 6144:8192]
        w2s = wbig[:, 8192:16384]

        xf = sb("xf", [P, D], F32)
        kb = sb("kb", [P, HID], BF16)
        vb = sb("vb", [P, HID], BF16)
        se0 = sb("se0", [P, HH * QC * C], BF16)
        se1 = sb("se1", [P, HH * QC * C // 2], BF16)
        se = [se0, se1]
        nbuf = sb("nbuf", [P, HID], F32)
        dbuf = sb("dbuf", [P, HID], F32)
        attnfm = sb("attnfm", [P, HID], BF16)
        xfm = attnfm
        r1 = sb("r1", [P, D], F32)
        sq = sb("sqs", [P, D], F32)
        assert TG <= D
        silu = sq
        t3b = sb("t3b", [P, TG], F32)
        ug = sb("ug", [P, TG], F32)
        ul = sb("ul", [P, TG], F32)
        ln1fm = sb("ln1fm", [P, 8 * TG], BF16)
        afm = sb("afm", [P, 8 * TG], BF16)
        acc = sb("acc", [P, NT * D], F32)
        cw = sb("cw", [P, NT * E], F32)
        gsc = sb("gsc", [P, E], F32)
        gtmp = sb("gtmp", [P, E], F32)
        gtmp2 = sb("gtmp2", [P, E], F32)
        colA = sb("colA", [P, 1], F32)
        colB = sb("colB", [P, 1], F32)
        colC = sb("colC", [P, 1], F32)
        colD = sb("colD", [P, 1], F32)

        outb = r1

        ps = [es.enter_context(nc.psum_tensor(f"ps{i}", [P, 1024], F32)) for i in range(4)]
        pgate = ps[2][:, 512:512 + E]

        sems = {e: es.enter_context(nc.semaphore(f"sem_{e}")) for e in ENGS}
        sch = Sched()
        block = es.enter_context(nc.Block())

        def ld(engq, dst, src, name, after=()):
            sch.op(engq, lambda eng, dst=dst, src=src: eng.dma_start(out=dst, in_=src),
                   after=after, name=name, dma=True)

        def tapst(nm, src, after):
            if taps and nm not in sch.byname:
                sch.op("sync", lambda s, src=src: s.dma_start(out=tap_t[nm], in_=src),
                       after=after, name=nm, dma=True)
                return nm
            return None

        ld("sync", bcol[:], bcol_in, "c_bcol")
        ld("sync", gbr[:], gb_in, "c_gb")
        ld("sync", eye[:], eye_in, "c_eye")
        ld("gpsimd", msk[:], msk_in, "c_msk")
        ld("gpsimd", gwT[:], gwT_in, "c_gw")
        ld("gpsimd", brow[:], brow_in, "c_brow")
        sch.op("vector", lambda v: v.memset(ones_r[:], 1.0), name="c_ones")

        def brow_s(i, js):
            return brow[0:1, i * 1024 + js * 512: i * 1024 + (js + 1) * 512]

        def w_src_qkv(w):
            return w.rearrange("(kc p) n -> p kc n", kc=8, p=P)

        def r3(ap, kc=8):
            return ap.rearrange("p (kc n) -> p kc n", kc=kc)

        def emit_group(g):
            pg = f"g{g}"
            aft_arena = (f"g{g-1}_ffn_mm_done",) if g else ("c_brow",)
            ld("gpsimd", r3(wqs), w_src_qkv(wq_in), f"{pg}_lwq", after=aft_arena)
            ld("gpsimd", r3(wks), w_src_qkv(wk_in), f"{pg}_lwk", after=aft_arena)
            ld("gpsimd", r3(wvs), w_src_qkv(wv_in), f"{pg}_lwv", after=aft_arena)
            ld("gpsimd", r3(wos), w_src_qkv(wo_in), f"{pg}_lwo",
               after=(f"g{g-1}_ao_done",) if g else ())

            for i in range(NT):
                t = f"{pg}t{i}"
                rows = slice((g * NT + i) * P, (g * NT + i + 1) * P)
                aft_xf = ()
                if i >= 1:
                    aft_xf = (f"{pg}t{i-1}_r1",)
                elif g > 0:
                    aft_xf = (f"g{g-1}t{NT-1}_r1",)
                ld("sync", xf[:], x_in[rows, :], f"{t}_lx", after=aft_xf)

                def tx(eng):
                    for c2 in range(8):
                        inst = eng.transpose(ps[3][:, c2 * P:(c2 + 1) * P],
                                             xf[:, c2 * P:(c2 + 1) * P], eye[:])
                    return inst
                aft = [f"{t}_lx", "c_eye"]
                if i > 0:
                    aft.append(f"{pg}t{i-1}_elnfm")
                elif g > 0:
                    aft.append(f"g{g-1}_ffn_done")
                sch.op("tensor", tx, after=aft, name=f"{t}_tx")
                sch.op("scalar", lambda s: s.copy(xfm[:], ps[3][:]),
                       after=(f"{t}_tx",), name=f"{t}_exfm")

                def qkv(eng):
                    for pi in range(3):
                        for js in range(2):
                            eng.matmul(ps[pi][:, js * 512:(js + 1) * 512], ones_r[0:1, :],
                                       brow_s(pi, js), start=True, stop=False)
                    for kc in range(8):
                        for pi, w in ((0, wqs), (1, wks), (2, wvs)):
                            for js in range(2):
                                inst = eng.matmul(
                                    ps[pi][:, js * 512:(js + 1) * 512],
                                    xfm[:, kc * P:(kc + 1) * P],
                                    w[:, kc * HID + js * 512: kc * HID + (js + 1) * 512],
                                    start=False, stop=(kc == 7))
                    return inst
                aft = [f"{t}_exfm", f"{pg}_lwq", f"{pg}_lwk", f"{pg}_lwv", "c_ones", "c_brow"]
                if i > 0:
                    aft += [f"{pg}t{i-1}_r1", f"{pg}t{i-1}_eqkv", f"{pg}t{i-1}_attnb",
                            f"{pg}t{i-1}_egate"]
                elif g > 0:
                    aft += [f"g{g-1}t{NT-1}_r1", f"g{g-1}t{NT-1}_eqkv",
                            f"g{g-1}t{NT-1}_attnb", f"g{g-1}t{NT-1}_egate"]
                sch.op("tensor", qkv, after=aft, name=f"{t}_qkv")

                def eqkv(s):
                    s.copy(kb[:], ps[1][:])
                    return s.copy(vb[:], ps[2][:])
                aft = [f"{t}_qkv"]
                if i:
                    aft.append(f"{pg}t{i-1}_attnb")
                elif g:
                    aft.append(f"g{g-1}t{NT-1}_attnb")
                sch.op("scalar", eqkv, after=aft, name=f"{t}_eqkv")
                if g == 0 and i == 0:
                    tapst("tap_kb", kb[:], (f"{t}_eqkv",))
                    tapst("tap_vb", vb[:], (f"{t}_eqkv",))

                CH_ORDER = [(2, 0, 0), (0, 0, 1), (2, 1, 0), (0, 1, 1),
                            (3, 0, 0), (1, 0, 1), (3, 1, 0), (1, 1, 1)]
                for n_, (c, hh, bsel) in enumerate(CH_ORDER):
                    kext = QC * (c + 1)
                    sebuf = se[bsel]
                    s4 = sebuf[:, :HH * QC * kext].rearrange(
                        "p (h q k) -> p h q k", h=HH, q=QC, k=kext)
                    hsl = slice(hh * HH, (hh + 1) * HH)
                    q_ap = ps[0][:, :HID].rearrange("p (h q) -> p h q", h=H)[:, hsl, c * QC:(c + 1) * QC] \
                        .unsqueeze(3).broadcast_to([P, HH, QC, kext])
                    k_ap = kb[:].rearrange("p (h k) -> p h k", h=H)[:, hsl, 0:kext] \
                        .unsqueeze(2).broadcast_to([P, HH, QC, kext])
                    v_ap = vb[:].rearrange("p (h k) -> p h k", h=H)[:, hsl, 0:kext] \
                        .unsqueeze(2).broadcast_to([P, HH, QC, kext])
                    aft = [f"{t}_eqkv", f"{t}_qkv"]
                    if n_ >= 2:
                        pc, ph, _ = CH_ORDER[n_ - 2]
                        aft.append(f"{t}_exp{pc}{ph}")
                    sch.op("vector",
                           lambda v, s4=s4, q_ap=q_ap, k_ap=k_ap:
                           v.tensor_tensor(s4, q_ap, k_ap, op=ALU.mult),
                           after=aft, name=f"{t}_s{c}{hh}")
                    sch.op("scalar",
                           lambda s, s4=s4: s.activation(s4, s4, AF.Exp, scale=0.125),
                           after=(f"{t}_s{c}{hh}",), name=f"{t}_exp{c}{hh}")
                    diag = s4[:, :, :, c * QC:(c + 1) * QC]
                    m_ap = msk[:].rearrange("p (q k) -> p q k", q=QC, k=QC) \
                        .unsqueeze(1).broadcast_to([P, HH, QC, QC])
                    sch.op("vector",
                           lambda v, diag=diag, m_ap=m_ap:
                           v.tensor_tensor(diag, diag, m_ap, op=ALU.mult),
                           after=(f"{t}_exp{c}{hh}", "c_msk"), name=f"{t}_msk{c}{hh}")
                    d_out = dbuf[:].rearrange("p (h q) -> p h q", h=H)[:, hsl, c * QC:(c + 1) * QC]
                    sch.op("vector",
                           lambda v, d_out=d_out, s4=s4:
                           v.tensor_reduce(d_out, s4, axis=AX.X, op=ALU.add),
                           name=f"{t}_dred{c}{hh}")
                    sch.op("vector",
                           lambda v, s4=s4, v_ap=v_ap:
                           v.tensor_tensor(s4, s4, v_ap, op=ALU.mult),
                           name=f"{t}_ev{c}{hh}")
                    n_out = nbuf[:].rearrange("p (h q) -> p h q", h=H)[:, hsl, c * QC:(c + 1) * QC]
                    sch.op("vector",
                           lambda v, n_out=n_out, s4=s4:
                           v.tensor_reduce(n_out, s4, axis=AX.X, op=ALU.add),
                           name=f"{t}_nred{c}{hh}")

                sch.op("vector", lambda v: v.reciprocal(dbuf[:], dbuf[:]),
                       name=f"{t}_dinv")
                sch.op("vector", lambda v: v.tensor_tensor(nbuf[:], nbuf[:], dbuf[:], op=ALU.mult),
                       name=f"{t}_attnb")
                if g == 0 and i == 0:
                    tapst("tap_dinv", dbuf[:], (f"{t}_attnb",))
                    tapst("tap_attn", nbuf[:], (f"{t}_attnb",))

                def tattn(eng):
                    for c2 in range(8):
                        inst = eng.transpose(ps[3][:, c2 * P:(c2 + 1) * P],
                                             nbuf[:, c2 * P:(c2 + 1) * P], eye[:])
                    return inst
                sch.op("tensor", tattn, after=(f"{t}_attnb", f"{t}_exfm"), name=f"{t}_tattn")
                sch.op("scalar", lambda s: s.copy(attnfm[:], ps[3][:]),
                       after=(f"{t}_tattn",), name=f"{t}_eattnfm")

                def attno(eng):
                    for js in range(2):
                        eng.matmul(ps[0][:, js * 512:(js + 1) * 512], ones_r[0:1, :],
                                   brow_s(3, js), start=True, stop=False)
                    for kc in range(8):
                        for js in range(2):
                            inst = eng.matmul(
                                ps[0][:, js * 512:(js + 1) * 512],
                                attnfm[:, kc * P:(kc + 1) * P],
                                wos[:, kc * D + js * 512: kc * D + (js + 1) * 512],
                                start=False, stop=(kc == 7))
                    return inst
                sch.op("tensor", attno, after=(f"{t}_eattnfm", f"{pg}_lwo"), name=f"{t}_ao")
                if i == NT - 1:
                    sch.alias(f"{pg}_ao_done", f"{t}_ao")

                aft_r1 = [f"{t}_ao", f"{t}_lx"]
                if i == 0 and g > 0:
                    aft_r1.append(f"g{g-1}o{NT-1}_st")
                sch.op("vector", lambda v: v.tensor_tensor(r1[:], ps[0][:, :D], xf[:], op=ALU.add),
                       after=aft_r1, name=f"{t}_r1")
                if g == 0 and i == 0:
                    tapst("tap_r1", r1[:], (f"{t}_r1",))

                sch.op("vector", lambda v: v.tensor_reduce(colA[:], r1[:], axis=AX.X, op=ALU.add),
                       name=f"{t}_s1")
                sch.op("scalar", lambda s: s.activation(sq[:], r1[:], AF.Square),
                       after=(f"{t}_r1",), name=f"{t}_sq")
                def ln_small(v):
                    v.tensor_reduce(colB[:], sq[:], axis=AX.X, op=ALU.add)
                    v.drain()
                    v.tensor_scalar_mul(colA[:], colA[:], 1.0 / D)
                    v.tensor_scalar_mul(colB[:], colB[:], 1.0 / D)
                    v.drain()
                    v.tensor_tensor(colC[:], colA[:], colA[:], op=ALU.mult)
                    v.drain()
                    v.tensor_tensor(colB[:], colB[:], colC[:], op=ALU.subtract)
                    v.drain()
                    return v.tensor_scalar_add(colB[:], colB[:], float(EPS))
                sch.op("vector", ln_small, after=(f"{t}_sq",), name=f"{t}_var")
                sch.op("scalar", lambda s: s.activation(colC[:], colB[:], AF.Sqrt),
                       after=(f"{t}_var",), name=f"{t}_sqrt")
                sch.op("vector", lambda v: v.reciprocal(colD[:], colC[:]),
                       after=(f"{t}_sqrt",), name=f"{t}_rstd")
                accsl = acc[:, i * D:(i + 1) * D]
                aft_ln1 = []
                if g > 0:
                    aft_ln1.append(f"g{g-1}o{i}_ln2")
                def ln1n(v, accsl=accsl):
                    v.tensor_scalar(accsl, r1[:], colA[:], None, op0=ALU.subtract)
                    return v.tensor_scalar(accsl, accsl, colD[:], None, op0=ALU.mult)
                sch.op("vector", ln1n, after=aft_ln1, name=f"{t}_ln1")
                if g == 0 and i == 0:
                    tapst("tap_ln1", acc[:, 0:D], (f"{t}_ln1",))

                def tln(eng, accsl=accsl):
                    for c2 in range(8):
                        inst = eng.transpose(ps[3][:, c2 * P:(c2 + 1) * P],
                                             accsl[:, c2 * P:(c2 + 1) * P], eye[:])
                    return inst
                sch.op("tensor", tln, after=(f"{t}_ln1", f"{t}_eattnfm"), name=f"{t}_tln")
                lnout = ln1fm[:].rearrange("p (dc t) -> p dc t", dc=8)[:, :, i * P:(i + 1) * P]
                aft_elnfm = [f"{t}_tln"]
                if i == 0 and g > 0:
                    aft_elnfm.append(f"g{g-1}_ffn_mm_done")
                sch.op("scalar", lambda s, lnout=lnout: s.copy(
                    lnout, ps[3][:].rearrange("p (dc t) -> p dc t", dc=8, t=P)),
                    after=tuple(aft_elnfm), name=f"{t}_elnfm")

                def gate(eng, i=i):
                    for kc in range(8):
                        inst = eng.matmul(
                            pgate,
                            ln1fm[:, kc * TG + i * P: kc * TG + (i + 1) * P],
                            gwT[:, kc * E:(kc + 1) * E],
                            start=(kc == 0), stop=(kc == 7))
                    return inst
                aft_gate = [f"{t}_elnfm", "c_gw"]
                if i:
                    aft_gate.append(f"{pg}t{i-1}_cw")
                elif g:
                    aft_gate.append(f"g{g-1}t{NT-1}_cw")
                sch.op("tensor", gate, after=aft_gate, name=f"{t}_gate")
                sch.op("scalar", lambda s: s.copy(gsc[:], pgate),
                       after=(f"{t}_gate",), name=f"{t}_egate")
                sch.op("scalar", lambda s: s.activation(gtmp[:], gsc[:], AF.Exp),
                       after=(f"{t}_egate",), name=f"{t}_gexp")
                cwsl = cw[:, i * E:(i + 1) * E]
                def top2(v, cwsl=cwsl):
                    colAb = colA[:].broadcast_to([P, E])
                    v.tensor_reduce(colB[:], gtmp[:], axis=AX.X, op=ALU.add)
                    v.drain()
                    v.reciprocal(colC[:], colB[:])
                    v.drain()
                    v.tensor_scalar(gsc[:], gtmp[:], colC[:], None, op0=ALU.mult)
                    v.drain()
                    v.tensor_tensor(gtmp[:], gsc[:], gbr[:], op=ALU.add)
                    v.drain()
                    v.reduce_max(colA[:], gtmp[:], axis=AX.X)
                    v.drain()
                    v.tensor_tensor(gtmp2[:], gtmp[:], colAb, op=ALU.is_equal)
                    v.drain()
                    v.scalar_tensor_tensor(gtmp[:], gtmp2[:], NEGBIG, gtmp[:], op0=ALU.mult, op1=ALU.add)
                    v.drain()
                    v.reduce_max(colA[:], gtmp[:], axis=AX.X)
                    v.drain()
                    v.tensor_tensor(gtmp[:], gtmp[:], colAb, op=ALU.is_equal)
                    v.drain()
                    v.tensor_tensor(gtmp[:], gtmp[:], gtmp2[:], op=ALU.add)
                    v.drain()
                    return v.tensor_tensor(cwsl, gsc[:], gtmp[:], op=ALU.mult)
                sch.op("vector", top2, after=(f"{t}_gexp", "c_gb"), name=f"{t}_cw")
                if g == 0 and i == 0:
                    tapst("tap_gsc", gsc[:], (f"{t}_cw",))
                    tapst("tap_cw", cw[:, 0:E], (f"{t}_cw",))

                if g == 0 and i == NT - 1:
                    tapst("tap_ln1fm", ln1fm[:], (f"{t}_elnfm",))

            def w13_src(base, pair):
                return base.rearrange("(kc p) f -> p kc f", kc=8, p=P)[:, :, pair * 256:(pair + 1) * 256]

            NTS = TG // TS
            for e_ in range(E + 1):
                x_ = f"{pg}e{e_}"
                w1b = w1_in[e_] if e_ < E else sw1_in
                w3b = w3_in[e_] if e_ < E else sw3_in
                w2b = (w2_in[e_] if e_ < E else sw2_in).rearrange("(fc p) d2 -> p fc d2", fc=8, p=P)

                aft = [f"{pg}_w2mm_{e_-1}_{NT-1}"] if e_ else [f"{pg}t{NT-1}_qkv"]
                ld("gpsimd", r3(w2s), w2b, f"{x_}_lw2", after=aft)
                ld("gpsimd", b2row[:], b2s_in[e_:e_ + 1, :], f"{x_}_lb2", after=aft)
                sch.op("vector", lambda v: v.tensor_scalar_mul(w2s, w2s, INV_ALPHA),
                       after=(f"{x_}_lw2",), name=f"{x_}_w2sc")

                if e_ < E:
                    b1o, b3o = 16 * e_, 128 + 16 * e_
                else:
                    b1o, b3o = 256, 272

                for m in range(4):
                    def hq(ee, mm):
                        return [f"{pg}e{ee}_h1g_{2*mm+1}", f"{pg}e{ee}_h3g_{2*mm+1}",
                                f"{pg}e{ee}_h1l_{2*mm+1}", f"{pg}e{ee}_h3l_{2*mm+1}"]
                    if m > 0:
                        aft_l = hq(e_, m - 1)
                    elif e_ > 0:
                        aft_l = hq(e_ - 1, 3)
                    else:
                        aft_l = [f"{pg}t{NT-1}_qkv"]
                    def w13_both(base, mm):
                        return base.rearrange("(kc p) (r q n) -> p q r kc n",
                                              kc=8, p=P, r=2, q=4)[:, mm]
                    w1gl = wbig[:, 0:4096].rearrange("p (r kc n) -> p r kc n", r=2, kc=8)
                    w3gl = wbig[:, 4096:8192].rearrange("p (r kc n) -> p r kc n", r=2, kc=8)
                    ld("gpsimd", w1gl, w13_both(w1b, m), f"{x_}_lw1g_{m}", after=aft_l)
                    sch.alias(f"{x_}_lw1l_{m}", f"{x_}_lw1g_{m}")
                    ld("gpsimd", w3gl, w13_both(w3b, m), f"{x_}_lw3g_{m}", after=aft_l)
                    sch.alias(f"{x_}_lw3l_{m}", f"{x_}_lw3g_{m}")

                    for j2 in range(2):
                        p_ = 2 * m + j2
                        fcg = p_
                        fcl = 8 + p_
                        aft_g = [f"{x_}_lw1g_{m}", f"{x_}_lw3g_{m}", f"{pg}t{NT-1}_elnfm"]
                        if p_ > 0:
                            aft_g.append(f"{x_}_ug_{p_-1}")
                        elif e_ > 0:
                            aft_g.append(f"{pg}_acc_{e_-1}_{NT-1}")
                        else:
                            aft_g += [f"{pg}t{NT-1}_r1", f"{pg}t{NT-1}_eqkv",
                                      f"{pg}t{NT-1}_attnb"]
                        def h1gm(eng, j2=j2):
                            for ts in range(NTS):
                                for kc in range(8):
                                    inst = eng.matmul(
                                        ps[0][:, ts * TS:(ts + 1) * TS],
                                        w1g[:, kc * 256 + j2 * 128: kc * 256 + (j2 + 1) * 128],
                                        ln1fm[:, kc * TG + ts * TS: kc * TG + (ts + 1) * TS],
                                        start=(kc == 0), stop=(kc == 7))
                            return inst
                        def h3gm(eng, j2=j2):
                            for ts in range(NTS):
                                for kc in range(8):
                                    inst = eng.matmul(
                                        ps[1][:, ts * TS:(ts + 1) * TS],
                                        w3g[:, kc * 256 + j2 * 128: kc * 256 + (j2 + 1) * 128],
                                        ln1fm[:, kc * TG + ts * TS: kc * TG + (ts + 1) * TS],
                                        start=(kc == 0), stop=(kc == 7))
                            return inst
                        sch.op("tensor", h1gm, after=aft_g, name=f"{x_}_h1g_{p_}")
                        sch.op("tensor", h3gm, after=aft_g, name=f"{x_}_h3g_{p_}")
                        aft_t3 = [f"{x_}_h3g_{p_}", "c_bcol"]
                        if p_ > 0:
                            aft_t3.append(f"{x_}_ul_{p_-1}")
                        elif e_ > 0:
                            aft_t3.append(f"{pg}e{e_-1}_ul_7")
                        sch.op("scalar",
                               lambda s, b3_ap=bcol[:, b3o + fcg: b3o + fcg + 1]:
                               s.activation(t3b[:, :TG], ps[1][:, :TG], AF.Identity, bias=b3_ap),
                               after=aft_t3, name=f"{x_}_t3g_{p_}")
                        aft_u = [f"{x_}_h1g_{p_}", f"{x_}_t3g_{p_}"]
                        if p_ > 0:
                            aft_u.append(f"{x_}_sil_{p_-1}")
                        elif e_ > 0:
                            aft_u.append(f"{pg}e{e_-1}_sil_7")
                        def ugf(v, b1_ap=bcol[:, b1o + fcg: b1o + fcg + 1]):
                            for c0 in range(0, TG, 512):
                                c1_ = min(c0 + 512, TG)
                                inst = v.scalar_tensor_tensor(
                                    ug[:, c0:c1_], ps[0][:, c0:c1_], b1_ap, t3b[:, c0:c1_],
                                    op0=ALU.add, op1=ALU.mult)
                            return inst
                        sch.op("vector", ugf, after=aft_u, name=f"{x_}_ug_{p_}")
                        aft_l2 = [f"{x_}_lw1l_{m}", f"{x_}_lw3l_{m}", f"{pg}t{NT-1}_elnfm"]
                        if p_ > 0:
                            aft_l2.append(f"{x_}_ul_{p_-1}")
                        elif e_ > 0:
                            aft_l2.append(f"{pg}e{e_-1}_ul_7")
                        else:
                            aft_l2 += [f"{pg}t{NT-1}_attnb", f"{pg}t{NT-1}_egate"]
                        def h1lm(eng, j2=j2):
                            for ts in range(NTS):
                                for kc in range(8):
                                    inst = eng.matmul(
                                        ps[2][:, ts * TS:(ts + 1) * TS],
                                        w1l[:, kc * 256 + j2 * 128: kc * 256 + (j2 + 1) * 128],
                                        ln1fm[:, kc * TG + ts * TS: kc * TG + (ts + 1) * TS],
                                        start=(kc == 0), stop=(kc == 7))
                            return inst
                        def h3lm(eng, j2=j2):
                            for ts in range(NTS):
                                for kc in range(8):
                                    inst = eng.matmul(
                                        ps[3][:, ts * TS:(ts + 1) * TS],
                                        w3l[:, kc * 256 + j2 * 128: kc * 256 + (j2 + 1) * 128],
                                        ln1fm[:, kc * TG + ts * TS: kc * TG + (ts + 1) * TS],
                                        start=(kc == 0), stop=(kc == 7))
                            return inst
                        sch.op("tensor", h1lm, after=aft_l2, name=f"{x_}_h1l_{p_}")
                        sch.op("tensor", h3lm, after=aft_l2, name=f"{x_}_h3l_{p_}")
                        sch.op("scalar",
                               lambda s, b3_ap=bcol[:, b3o + fcl: b3o + fcl + 1]:
                               s.activation(t3b[:, :TG], ps[3][:, :TG], AF.Identity, bias=b3_ap),
                               after=(f"{x_}_h3l_{p_}", f"{x_}_ug_{p_}"), name=f"{x_}_t3l_{p_}")
                        def ulf(v, b1_ap=bcol[:, b1o + fcl: b1o + fcl + 1]):
                            for c0 in range(0, TG, 512):
                                c1_ = min(c0 + 512, TG)
                                inst = v.scalar_tensor_tensor(
                                    ul[:, c0:c1_], ps[2][:, c0:c1_], b1_ap, t3b[:, c0:c1_],
                                    op0=ALU.add, op1=ALU.mult)
                            return inst
                        sch.op("vector", ulf, after=(f"{x_}_h1l_{p_}", f"{x_}_t3l_{p_}"),
                               name=f"{x_}_ul_{p_}")
                        sch.op("vector",
                               lambda v: v.tensor_scalar(ug[:, :TG], ug[:, :TG], LIMIT, None, op0=ALU.min),
                               name=f"{x_}_gmin_{p_}")
                        sch.op("scalar", lambda s: s.activation(silu[:, :TG], ug[:, :TG], AF.Silu, scale=ALPHA),
                               after=(f"{x_}_gmin_{p_}",), name=f"{x_}_sil_{p_}")
                        sch.op("vector",
                               lambda v: v.tensor_scalar(ul[:, :TG], ul[:, :TG], -LIMIT, LIMIT,
                                                         op0=ALU.max, op1=ALU.min),
                               name=f"{x_}_lc_{p_}")
                        if g == 0 and e_ == 1 and p_ == 0:
                            _tp = tapst("tap_afm", afm[:], (f"{pg}e0_a_7",))
                            if _tp:
                                aft_a_extra = (_tp,)
                        atgt = afm[:, fcg * TG:(fcg + 1) * TG]
                        aft_a = [f"{x_}_sil_{p_}"]
                        if e_ > 0:
                            aft_a.append(f"{pg}_w2mm_{e_-1}_{NT-1}")
                        if taps and g == 0 and e_ == 1 and p_ == 0 and "tap_afm" in sch.byname:
                            aft_a.append("tap_afm")
                        elif g > 0:
                            aft_a.append(f"g{g-1}_w2mm_{E}_{NT-1}")
                        sch.op("vector",
                               lambda v, atgt=atgt: v.scalar_tensor_tensor(
                                   atgt, ul[:, :TG], 1.0, silu[:, :TG], op0=ALU.add, op1=ALU.mult),
                               after=aft_a, name=f"{x_}_a_{p_}")


                for i in range(NT):
                    yp = ps[0] if i % 2 == 0 else ps[1]
                    def w2mm(eng, yp=yp, i=i):
                        for ds in range(2):
                            eng.matmul(yp[:, ds * 512:(ds + 1) * 512], ones_r[0:1, :],
                                       b2row[0:1, ds * 512:(ds + 1) * 512], start=True, stop=False)
                            for fc in range(8):
                                inst = eng.matmul(
                                    yp[:, ds * 512:(ds + 1) * 512],
                                    afm[:, fc * TG + i * P: fc * TG + (i + 1) * P],
                                    w2s[:, fc * D + ds * 512: fc * D + (ds + 1) * 512],
                                    start=False, stop=(fc == 7))
                        return inst
                    aft = [f"{x_}_a_7", f"{x_}_w2sc", f"{x_}_lb2"]
                    if i >= 2:
                        aft.append(f"{x_}_acc_{i-2}")
                    elif e_ == 0:
                        aft += [f"{x_}_ug_7", f"{x_}_t3g_7"]
                    else:
                        aft.append(f"{pg}_acc_{e_-1}_{NT-2+i}")
                    sch.op("tensor", w2mm, after=aft, name=f"{x_}_w2mm_{i}")
                    if i == NT - 1:
                        sch.alias(f"{pg}_w2mm_{e_}_{NT-1}", f"{x_}_w2mm_{i}")
                    accsl = acc[:, i * D:(i + 1) * D]
                    if e_ < E:
                        cwcol = cw[:, i * E + e_: i * E + e_ + 1]
                        def accf(v, yp=yp, accsl=accsl, cwcol=cwcol):
                            for c0 in range(0, D, 512):
                                v.scalar_tensor_tensor(
                                    silu[:, c0:c0 + 512], yp[:, c0:c0 + 512], cwcol,
                                    accsl[:, c0:c0 + 512], op0=ALU.mult, op1=ALU.add)
                            return v.tensor_copy(accsl, silu[:, :D])
                    else:
                        def accf(v, yp=yp, accsl=accsl):
                            return v.tensor_tensor(accsl, accsl, yp[:, :D], op=ALU.add)
                    sch.op("vector", accf, after=(f"{x_}_w2mm_{i}",), name=f"{pg}_acc_{e_}_{i}")
                    sch.alias(f"{x_}_acc_{i}", f"{pg}_acc_{e_}_{i}")
                    if g == 0 and e_ == E and i == 0:
                        tapst("tap_acc", acc[:, 0:D], (f"{pg}_acc_{e_}_{i}",))

            sch.alias(f"{pg}_ffn_mm_done", f"{pg}_w2mm_{E}_{NT-1}")
            sch.alias(f"{pg}_ffn_done", f"{pg}_acc_{E}_{NT-1}")

            for i in range(NT):
                t = f"{pg}o{i}"
                accsl = acc[:, i * D:(i + 1) * D]
                sch.op("vector", lambda v, accsl=accsl: v.tensor_reduce(colA[:], accsl, axis=AX.X, op=ALU.add),
                       after=(f"{pg}_acc_{E}_{i}",), name=f"{t}_s1")
                sch.op("scalar", lambda s, accsl=accsl: s.activation(sq[:], accsl, AF.Square),
                       after=(f"{pg}_acc_{E}_{i}",), name=f"{t}_sq")
                def ln_small2(v):
                    v.tensor_reduce(colB[:], sq[:], axis=AX.X, op=ALU.add)
                    v.drain()
                    v.tensor_scalar_mul(colA[:], colA[:], 1.0 / D)
                    v.tensor_scalar_mul(colB[:], colB[:], 1.0 / D)
                    v.drain()
                    v.tensor_tensor(colC[:], colA[:], colA[:], op=ALU.mult)
                    v.drain()
                    v.tensor_tensor(colB[:], colB[:], colC[:], op=ALU.subtract)
                    v.drain()
                    return v.tensor_scalar_add(colB[:], colB[:], float(EPS))
                sch.op("vector", ln_small2, after=(f"{t}_sq",), name=f"{t}_var")
                sch.op("scalar", lambda s: s.activation(colC[:], colB[:], AF.Sqrt),
                       after=(f"{t}_var",), name=f"{t}_sqrt")
                sch.op("vector", lambda v: v.reciprocal(colD[:], colC[:]),
                       after=(f"{t}_sqrt",), name=f"{t}_rstd")
                aft = [f"{t}_rstd"]
                if i >= 1:
                    aft.append(f"{pg}o{i-1}_st")
                elif g > 0:
                    aft.append(f"g{g-1}o{NT-1}_st")
                def ln2n(v, accsl=accsl):
                    v.tensor_scalar(outb[:], accsl, colA[:], None, op0=ALU.subtract)
                    return v.tensor_scalar(outb[:], outb[:], colD[:], None, op0=ALU.mult)
                sch.op("vector", ln2n, after=aft, name=f"{t}_ln2")
                rows = slice((g * NT + i) * P, (g * NT + i + 1) * P)
                sch.op("sync", lambda s, rows=rows: s.dma_start(out=out_t[rows, :], in_=outb[:]),
                       after=(f"{t}_ln2",), name=f"{t}_st", dma=True)

        for g in range(NG):
            emit_group(g)

        sch.op("sync", lambda s: s.nop(),
               after=tuple(f"g{NG-1}o{i}_st" for i in range(NT)), name="fin")

        if max_ops is not None:
            sch.ops = sch.ops[:max_ops]
        sch.build(block, sems)
    return nc


_NC_CACHE = {}


def _get_nc(TPC=4096, TG=1024):
    key = (TPC, TG)
    if key not in _NC_CACHE:
        _NC_CACHE[key] = build_nc(TPC, TG)
    return _NC_CACHE[key]


def prep_weights(kw):
    bf = ml_dtypes.bfloat16
    f32 = np.float32
    d = {}
    d["wq_in"] = np.ascontiguousarray(np.asarray(kw["L0_attn_Wq"], f32))
    d["wk_in"] = np.ascontiguousarray(np.asarray(kw["L0_attn_Wk"], f32))
    d["wv_in"] = np.ascontiguousarray(np.asarray(kw["L0_attn_Wv"], f32))
    d["wo_in"] = np.ascontiguousarray(np.asarray(kw["L0_attn_Wo"], f32))

    def permute_w13(w):
        wt = np.swapaxes(np.asarray(w, f32), -1, -2)
        return np.ascontiguousarray(np.concatenate([wt[..., 0::2], wt[..., 1::2]], axis=-1))

    d["w1_in"] = permute_w13(kw["L0_ffn_w1"])
    d["w3_in"] = permute_w13(kw["L0_ffn_w3"])
    d["sw1_in"] = permute_w13(kw["L0_ffn_sw1"])
    d["sw3_in"] = permute_w13(kw["L0_ffn_sw3"])
    d["w2_in"] = np.ascontiguousarray(np.swapaxes(np.asarray(kw["L0_ffn_w2"], f32), -1, -2))
    d["sw2_in"] = np.ascontiguousarray(np.asarray(kw["L0_ffn_sw2"], f32).T)

    brow = np.concatenate([
        np.asarray(kw["L0_attn_bq"], f32), np.asarray(kw["L0_attn_bk"], f32),
        np.asarray(kw["L0_attn_bv"], f32), np.asarray(kw["L0_attn_bo"], f32)])
    d["brow_in"] = np.ascontiguousarray(brow.reshape(1, 4 * 1024))
    d["b2s_in"] = np.ascontiguousarray(np.concatenate(
        [np.asarray(kw["L0_ffn_b2"], f32), np.asarray(kw["L0_ffn_sb2"], f32).reshape(1, -1)], axis=0))

    def colify(b):
        b = np.asarray(b, f32)
        bp = np.concatenate([b[0::2], b[1::2]])
        return np.ascontiguousarray(bp.reshape(16, 128).T)

    b1 = np.asarray(kw["L0_ffn_b1"], f32)
    b3 = np.asarray(kw["L0_ffn_b3"], f32)
    cols = [colify(b1[e]) for e in range(E)] + [colify(b3[e]) for e in range(E)] \
        + [colify(kw["L0_ffn_sb1"]), colify(kw["L0_ffn_sb3"])]
    d["bcol_in"] = np.ascontiguousarray(np.concatenate(cols, axis=1))

    gwT = np.asarray(kw["L0_ffn_gate_w"], f32).T.reshape(8, 128, E).transpose(1, 0, 2)
    d["gwT_in"] = np.ascontiguousarray(gwT.reshape(128, 64))
    d["gb_in"] = np.ascontiguousarray(np.broadcast_to(np.asarray(kw["L0_ffn_gate_b"], f32), (P, E)))

    d["eye_in"] = np.eye(P, dtype=f32)
    m = np.tril(np.ones((QC, QC), dtype=np.float32))
    d["msk_in"] = np.ascontiguousarray(np.broadcast_to(m.reshape(1, QC * QC), (P, QC * QC)).astype(bf))
    return d


def kernel(**inputs):
    x = np.asarray(inputs["L0_attn_x"], dtype=np.float32).reshape(-1, D)
    n_cores = 8
    shards = np.split(np.ascontiguousarray(x), n_cores, axis=0)
    wmap = prep_weights(inputs)
    nc = _get_nc(shards[0].shape[0])
    in_maps = []
    for s in shards:
        m = dict(wmap)
        m["x_in"] = s
        in_maps.append(m)
    res = run_bass_kernel_spmd(nc, in_maps, core_ids=list(range(n_cores)))
    out = np.concatenate([np.asarray(r["out"]) for r in res.results], axis=0)
    return out.reshape(2, 16384, D)


if __name__ == "__main__":
    nc = build_nc()
    print("build ok")


# version 08 (final)
# speedup vs baseline: 1.0847x; 1.0669x over previous
import sys
import numpy as np
import ml_dtypes

for _p in ("/opt/trn_rl_repo",):
    if _p not in sys.path:
        sys.path.insert(0, _p)

import concourse.bass as bass
import concourse.mybir as mybir
from concourse.bass_utils import run_bass_kernel_spmd
from contextlib import ExitStack

F32 = mybir.dt.float32
BF16 = mybir.dt.bfloat16
AF = mybir.ActivationFunctionType
ALU = mybir.AluOpType
AX = mybir.AxisListType

P = 128
D = 1024
H, C = 16, 64
HH = 8
HID = 1024
E = 8
F2 = 2048
F = 1024
NCHUNK = 4
QC = C // NCHUNK
ALPHA = 1.702
INV_ALPHA = 1.0 / ALPHA
LIMIT = 7.0
EPS = 1e-5
NEGBIG = -1.0e30

ENGS = ("tensor", "scalar", "vector", "sync", "gpsimd")


class Sched:

    def __init__(self):
        self.ops = []
        self.byname = {}
        self._last_dma = {}

    def op(self, engine, emit, after=(), name=None, dma=False):
        i = len(self.ops)
        after = tuple(a for a in after if a)
        if dma:
            prev = self._last_dma.get(engine)
            if prev is not None and prev not in after:
                after = after + (prev,)
            assert name, "dma ops must be named"
            self._last_dma[engine] = name
        self.ops.append(dict(engine=engine, emit=emit, after=after,
                             name=name, dma=dma, idx=i))
        if name:
            assert name not in self.byname, f"dup event {name}"
            self.byname[name] = i
        return name

    def alias(self, new, old):
        self.byname[new] = self.byname[old]

    def build(self, block, sems):
        need_sig = set(o["idx"] for o in self.ops if o["dma"])
        for o in self.ops:
            for a in o["after"]:
                assert a in self.byname, f"unknown event {a!r} needed by {o['name'] or o['idx']}"
                j = self.byname[a]
                assert j < o["idx"], f"event {a} not before {o['name'] or o['idx']}"
                if self.ops[j]["engine"] != o["engine"] or self.ops[j]["dma"]:
                    need_sig.add(j)
        counters = {e: 0 for e in ENGS}
        sigval = {}
        for o in self.ops:
            if o["idx"] in need_sig:
                counters[o["engine"]] += 16 if o["dma"] else 1
                sigval[o["idx"]] = counters[o["engine"]]
        self.final_counts = dict(counters)
        for eng_name in ENGS:
            eng_ops = [o for o in self.ops if o["engine"] == eng_name]
            if not eng_ops:
                continue

            def mk(eng_ops=eng_ops, eng_name=eng_name):
                def body(eng):
                    hw = {e: 0 for e in ENGS}
                    for o in eng_ops:
                        for a in o["after"]:
                            j = self.byname[a]
                            pe = self.ops[j]["engine"]
                            if pe == eng_name and not self.ops[j]["dma"]:
                                continue
                            v = sigval[j]
                            if v > hw[pe]:
                                eng.wait_ge(sems[pe], v)
                                hw[pe] = v
                        inst = o["emit"](eng)
                        if o["idx"] in need_sig:
                            assert inst is not None, f"op {o['name']} returned no instruction"
                            inst.then_inc(sems[o["engine"]], 16 if o["dma"] else 1)
                return body
            getattr(block, eng_name)(mk())


def build_nc(TPC=4096, TG=1024, max_ops=None, taps=False):
    NT = TG // P
    NG = TPC // TG
    assert NT * P == TG and NG * TG == TPC
    TS = min(512, TG)

    nc = bass.Bass("TRN2", target_bir_lowering=False, debug=False,
                   detect_race_conditions=False)
    x_in = nc.dram_tensor("x_in", [TPC, D], F32, kind="ExternalInput").ap()
    wq_in = nc.dram_tensor("wq_in", [D, HID], F32, kind="ExternalInput").ap()
    wk_in = nc.dram_tensor("wk_in", [D, HID], F32, kind="ExternalInput").ap()
    wv_in = nc.dram_tensor("wv_in", [D, HID], F32, kind="ExternalInput").ap()
    wo_in = nc.dram_tensor("wo_in", [HID, D], F32, kind="ExternalInput").ap()
    brow_in = nc.dram_tensor("brow_in", [1, 4 * 1024], F32, kind="ExternalInput").ap()
    b2s_in = nc.dram_tensor("b2s_in", [E + 1, D], F32, kind="ExternalInput").ap()
    bcol_in = nc.dram_tensor("bcol_in", [P, 288], F32, kind="ExternalInput").ap()
    gwT_in = nc.dram_tensor("gwT_in", [P, 64], F32, kind="ExternalInput").ap()
    gb_in = nc.dram_tensor("gb_in", [P, E], F32, kind="ExternalInput").ap()
    w1_in = nc.dram_tensor("w1_in", [E, D, F2], F32, kind="ExternalInput").ap()
    w3_in = nc.dram_tensor("w3_in", [E, D, F2], F32, kind="ExternalInput").ap()
    w2_in = nc.dram_tensor("w2_in", [E, F, D], F32, kind="ExternalInput").ap()
    sw1_in = nc.dram_tensor("sw1_in", [D, F2], F32, kind="ExternalInput").ap()
    sw3_in = nc.dram_tensor("sw3_in", [D, F2], F32, kind="ExternalInput").ap()
    sw2_in = nc.dram_tensor("sw2_in", [F, D], F32, kind="ExternalInput").ap()
    eye_in = nc.dram_tensor("eye_in", [P, P], F32, kind="ExternalInput").ap()
    msk_in = nc.dram_tensor("msk_in", [P, QC * QC], BF16, kind="ExternalInput").ap()
    out_t = nc.dram_tensor("out", [TPC, D], F32, kind="ExternalOutput").ap()
    tap_t = {}
    if taps:
        for nm, cols, dt in [("tap_kb", 1024, BF16), ("tap_vb", 1024, BF16),
                             ("tap_dinv", 1024, F32), ("tap_attn", 1024, F32),
                             ("tap_r1", 1024, F32), ("tap_ln1", 1024, F32),
                             ("tap_gsc", E, F32), ("tap_cw", E, F32),
                             ("tap_acc", 1024, F32),
                             ("tap_ln1fm", 8 * TG, BF16), ("tap_afm", 8 * TG, BF16)]:
            tap_t[nm] = nc.dram_tensor(nm, [P, cols], dt, kind="ExternalOutput").ap()

    es = ExitStack()
    sb = lambda name, shape, dt: es.enter_context(nc.sbuf_tensor(name, shape, dt))

    with es:
        eye = sb("eye", [P, P], F32)
        msk = sb("msk", [P, QC * QC], BF16)
        gwT = sb("gwT", [P, 64], BF16)
        gbr = sb("gbr", [P, E], F32)
        brow = sb("brow", [1, 4 * 1024], BF16)
        b2row = sb("b2row", [1, D], BF16)
        bcol = sb("bcol", [P, 288], F32)
        ones_r = sb("ones_r", [1, P], BF16)

        wbig = sb("wbig", [P, 4 * 8192], BF16)
        wqs = wbig[:, 0:8192]
        wks = wbig[:, 8192:16384]
        wvs = wbig[:, 16384:24576]
        wos = wbig[:, 24576:32768]
        w1g = wbig[:, 0:2048]
        w1l = wbig[:, 2048:4096]
        w3g = wbig[:, 4096:6144]
        w3l = wbig[:, 6144:8192]
        w2s = wbig[:, 8192:16384]

        xf = sb("xf", [P, D], F32)
        kb = sb("kb", [P, HID], BF16)
        vb = sb("vb", [P, HID], BF16)
        se0 = sb("se0", [P, HH * QC * C], BF16)
        se1 = sb("se1", [P, HH * QC * C // 2], BF16)
        se = [se0, se1]
        nbuf = sb("nbuf", [P, HID], F32)
        dbuf = sb("dbuf", [P, HID], F32)
        attnfm = sb("attnfm", [P, HID], BF16)
        xfm = attnfm
        r1 = sb("r1", [P, D], F32)
        sq = sb("sqs", [P, D], F32)
        assert TG <= D
        silu = sq
        t3b = sb("t3b", [P, TG], F32)
        ug = sb("ug", [P, TG], F32)
        ul = sb("ul", [P, TG], F32)
        ln1fm = sb("ln1fm", [P, 8 * TG], BF16)
        afm = sb("afm", [P, 8 * TG], BF16)
        acc = sb("acc", [P, NT * D], F32)
        cw = sb("cw", [P, NT * E], F32)
        gsc = sb("gsc", [P, E], F32)
        gtmp = sb("gtmp", [P, E], F32)
        gtmp2 = sb("gtmp2", [P, E], F32)
        colA = sb("colA", [P, 1], F32)
        colB = sb("colB", [P, 1], F32)
        colC = sb("colC", [P, 1], F32)
        colD = sb("colD", [P, 1], F32)

        outb = r1

        ps = [es.enter_context(nc.psum_tensor(f"ps{i}", [P, 1024], F32)) for i in range(4)]
        pgate = ps[2][:, 512:512 + E]

        sems = {e: es.enter_context(nc.semaphore(f"sem_{e}")) for e in ENGS}
        sch = Sched()
        block = es.enter_context(nc.Block())

        def ld(engq, dst, src, name, after=()):
            sch.op(engq, lambda eng, dst=dst, src=src: eng.dma_start(out=dst, in_=src),
                   after=after, name=name, dma=True)

        def tapst(nm, src, after):
            if taps and nm not in sch.byname:
                sch.op("sync", lambda s, src=src: s.dma_start(out=tap_t[nm], in_=src),
                       after=after, name=nm, dma=True)
                return nm
            return None

        ld("sync", bcol[:], bcol_in, "c_bcol")
        ld("sync", gbr[:], gb_in, "c_gb")
        ld("sync", eye[:], eye_in, "c_eye")
        ld("gpsimd", msk[:], msk_in, "c_msk")
        ld("gpsimd", gwT[:], gwT_in, "c_gw")
        ld("gpsimd", brow[:], brow_in, "c_brow")
        sch.op("vector", lambda v: v.memset(ones_r[:], 1.0), name="c_ones")

        def brow_s(i, js):
            return brow[0:1, i * 1024 + js * 512: i * 1024 + (js + 1) * 512]

        def w_src_qkv(w):
            return w.rearrange("(kc p) n -> p kc n", kc=8, p=P)

        def r3(ap, kc=8):
            return ap.rearrange("p (kc n) -> p kc n", kc=kc)

        def emit_group(g):
            pg = f"g{g}"
            aft_arena = (f"g{g-1}_ffn_mm_done",) if g else ("c_brow",)
            ld("gpsimd", r3(wqs), w_src_qkv(wq_in), f"{pg}_lwq", after=aft_arena)
            ld("gpsimd", r3(wks), w_src_qkv(wk_in), f"{pg}_lwk", after=aft_arena)
            ld("gpsimd", r3(wvs), w_src_qkv(wv_in), f"{pg}_lwv", after=aft_arena)
            ld("gpsimd", r3(wos), w_src_qkv(wo_in), f"{pg}_lwo",
               after=(f"g{g-1}_ao_done",) if g else ())

            for i in range(NT):
                t = f"{pg}t{i}"
                rows = slice((g * NT + i) * P, (g * NT + i + 1) * P)
                aft_xf = ()
                if i >= 1:
                    aft_xf = (f"{pg}t{i-1}_r1",)
                elif g > 0:
                    aft_xf = (f"g{g-1}t{NT-1}_r1",)
                ld("sync", xf[:], x_in[rows, :], f"{t}_lx", after=aft_xf)

                def tx(eng):
                    for c2 in range(8):
                        inst = eng.transpose(ps[3][:, c2 * P:(c2 + 1) * P],
                                             xf[:, c2 * P:(c2 + 1) * P], eye[:])
                    return inst
                aft = [f"{t}_lx", "c_eye"]
                if i > 0:
                    aft.append(f"{pg}t{i-1}_elnfm")
                elif g > 0:
                    aft.append(f"g{g-1}_ffn_done")
                sch.op("tensor", tx, after=aft, name=f"{t}_tx")
                sch.op("scalar", lambda s: s.copy(xfm[:], ps[3][:]),
                       after=(f"{t}_tx",), name=f"{t}_exfm")

                def qkv(eng):
                    for pi in range(3):
                        for js in range(2):
                            eng.matmul(ps[pi][:, js * 512:(js + 1) * 512], ones_r[0:1, :],
                                       brow_s(pi, js), start=True, stop=False)
                    for kc in range(8):
                        for pi, w in ((0, wqs), (1, wks), (2, wvs)):
                            for js in range(2):
                                inst = eng.matmul(
                                    ps[pi][:, js * 512:(js + 1) * 512],
                                    xfm[:, kc * P:(kc + 1) * P],
                                    w[:, kc * HID + js * 512: kc * HID + (js + 1) * 512],
                                    start=False, stop=(kc == 7))
                    return inst
                aft = [f"{t}_exfm", f"{pg}_lwq", f"{pg}_lwk", f"{pg}_lwv", "c_ones", "c_brow"]
                if i > 0:
                    aft += [f"{pg}t{i-1}_r1", f"{pg}t{i-1}_eqkv", f"{pg}t{i-1}_attnb",
                            f"{pg}t{i-1}_egate"]
                elif g > 0:
                    aft += [f"g{g-1}t{NT-1}_r1", f"g{g-1}t{NT-1}_eqkv",
                            f"g{g-1}t{NT-1}_attnb", f"g{g-1}t{NT-1}_egate"]
                sch.op("tensor", qkv, after=aft, name=f"{t}_qkv")

                def eqkv(s):
                    s.copy(kb[:], ps[1][:])
                    return s.copy(vb[:], ps[2][:])
                aft = [f"{t}_qkv"]
                if i:
                    aft.append(f"{pg}t{i-1}_attnb")
                elif g:
                    aft.append(f"g{g-1}t{NT-1}_attnb")
                sch.op("scalar", eqkv, after=aft, name=f"{t}_eqkv")
                if g == 0 and i == 0:
                    tapst("tap_kb", kb[:], (f"{t}_eqkv",))
                    tapst("tap_vb", vb[:], (f"{t}_eqkv",))

                CH_ORDER = [(2, 0, 0), (0, 0, 1), (2, 1, 0), (0, 1, 1),
                            (3, 0, 0), (1, 0, 1), (3, 1, 0), (1, 1, 1)]
                def ch_views(n_):
                    c, hh, bsel = CH_ORDER[n_]
                    kext = QC * (c + 1)
                    s4 = se[bsel][:, :HH * QC * kext].rearrange(
                        "p (h q k) -> p h q k", h=HH, q=QC, k=kext)
                    hsl = slice(hh * HH, (hh + 1) * HH)
                    return c, hh, kext, s4, hsl

                def emit_s(n_):
                    c, hh, kext, s4, hsl = ch_views(n_)
                    q_ap = ps[0][:, :HID].rearrange("p (h q) -> p h q", h=H)[:, hsl, c * QC:(c + 1) * QC] \
                        .unsqueeze(3).broadcast_to([P, HH, QC, kext])
                    k_ap = kb[:].rearrange("p (h k) -> p h k", h=H)[:, hsl, 0:kext] \
                        .unsqueeze(2).broadcast_to([P, HH, QC, kext])
                    aft = [f"{t}_eqkv", f"{t}_qkv"]
                    if n_ >= 2:
                        pc, ph, _ = CH_ORDER[n_ - 2]
                        aft.append(f"{t}_exp{pc}{ph}")
                    sch.op("vector",
                           lambda v, s4=s4, q_ap=q_ap, k_ap=k_ap:
                           v.tensor_tensor(s4, q_ap, k_ap, op=ALU.mult),
                           after=aft, name=f"{t}_s{c}{hh}")
                    sch.op("scalar",
                           lambda s, s4=s4: s.activation(s4, s4, AF.Exp, scale=0.125),
                           after=(f"{t}_s{c}{hh}",), name=f"{t}_exp{c}{hh}")

                def emit_tail(n_):
                    c, hh, kext, s4, hsl = ch_views(n_)
                    v_ap = vb[:].rearrange("p (h k) -> p h k", h=H)[:, hsl, 0:kext] \
                        .unsqueeze(2).broadcast_to([P, HH, QC, kext])
                    diag = s4[:, :, :, c * QC:(c + 1) * QC]
                    m_ap = msk[:].rearrange("p (q k) -> p q k", q=QC, k=QC) \
                        .unsqueeze(1).broadcast_to([P, HH, QC, QC])
                    sch.op("vector",
                           lambda v, diag=diag, m_ap=m_ap:
                           v.tensor_tensor(diag, diag, m_ap, op=ALU.mult),
                           after=(f"{t}_exp{c}{hh}", "c_msk"), name=f"{t}_msk{c}{hh}")
                    d_out = dbuf[:].rearrange("p (h q) -> p h q", h=H)[:, hsl, c * QC:(c + 1) * QC]
                    sch.op("vector",
                           lambda v, d_out=d_out, s4=s4:
                           v.tensor_reduce(d_out, s4, axis=AX.X, op=ALU.add),
                           name=f"{t}_dred{c}{hh}")
                    sch.op("vector",
                           lambda v, s4=s4, v_ap=v_ap:
                           v.tensor_tensor(s4, s4, v_ap, op=ALU.mult),
                           name=f"{t}_ev{c}{hh}")
                    n_out = nbuf[:].rearrange("p (h q) -> p h q", h=H)[:, hsl, c * QC:(c + 1) * QC]
                    sch.op("vector",
                           lambda v, n_out=n_out, s4=s4:
                           v.tensor_reduce(n_out, s4, axis=AX.X, op=ALU.add),
                           name=f"{t}_nred{c}{hh}")

                emit_s(0)
                for n_ in range(len(CH_ORDER)):
                    if n_ + 1 < len(CH_ORDER):
                        emit_s(n_ + 1)
                    emit_tail(n_)

                sch.op("vector", lambda v: v.reciprocal(dbuf[:], dbuf[:]),
                       name=f"{t}_dinv")
                sch.op("vector", lambda v: v.tensor_tensor(nbuf[:], nbuf[:], dbuf[:], op=ALU.mult),
                       name=f"{t}_attnb")
                if g == 0 and i == 0:
                    tapst("tap_dinv", dbuf[:], (f"{t}_attnb",))
                    tapst("tap_attn", nbuf[:], (f"{t}_attnb",))

                def tattn(eng):
                    for c2 in range(8):
                        inst = eng.transpose(ps[3][:, c2 * P:(c2 + 1) * P],
                                             nbuf[:, c2 * P:(c2 + 1) * P], eye[:])
                    return inst
                sch.op("tensor", tattn, after=(f"{t}_attnb", f"{t}_exfm"), name=f"{t}_tattn")
                sch.op("scalar", lambda s: s.copy(attnfm[:], ps[3][:]),
                       after=(f"{t}_tattn",), name=f"{t}_eattnfm")

                def attno(eng):
                    for js in range(2):
                        eng.matmul(ps[0][:, js * 512:(js + 1) * 512], ones_r[0:1, :],
                                   brow_s(3, js), start=True, stop=False)
                    for kc in range(8):
                        for js in range(2):
                            inst = eng.matmul(
                                ps[0][:, js * 512:(js + 1) * 512],
                                attnfm[:, kc * P:(kc + 1) * P],
                                wos[:, kc * D + js * 512: kc * D + (js + 1) * 512],
                                start=False, stop=(kc == 7))
                    return inst
                sch.op("tensor", attno, after=(f"{t}_eattnfm", f"{pg}_lwo"), name=f"{t}_ao")
                if i == NT - 1:
                    sch.alias(f"{pg}_ao_done", f"{t}_ao")

                aft_r1 = [f"{t}_ao", f"{t}_lx"]
                if i == 0 and g > 0:
                    aft_r1.append(f"g{g-1}o{NT-1}_st")
                sch.op("vector", lambda v: v.tensor_tensor(r1[:], ps[0][:, :D], xf[:], op=ALU.add),
                       after=aft_r1, name=f"{t}_r1")
                if g == 0 and i == 0:
                    tapst("tap_r1", r1[:], (f"{t}_r1",))

                sch.op("vector", lambda v: v.tensor_reduce(colA[:], r1[:], axis=AX.X, op=ALU.add),
                       name=f"{t}_s1")
                sch.op("scalar", lambda s: s.activation(sq[:], r1[:], AF.Square),
                       after=(f"{t}_r1",), name=f"{t}_sq")
                def ln_small(v):
                    v.tensor_reduce(colB[:], sq[:], axis=AX.X, op=ALU.add)
                    v.drain()
                    v.tensor_scalar_mul(colA[:], colA[:], 1.0 / D)
                    v.tensor_scalar_mul(colB[:], colB[:], 1.0 / D)
                    v.drain()
                    v.tensor_tensor(colC[:], colA[:], colA[:], op=ALU.mult)
                    v.drain()
                    v.tensor_tensor(colB[:], colB[:], colC[:], op=ALU.subtract)
                    v.drain()
                    return v.tensor_scalar_add(colB[:], colB[:], float(EPS))
                sch.op("vector", ln_small, after=(f"{t}_sq",), name=f"{t}_var")
                sch.op("scalar", lambda s: s.activation(colC[:], colB[:], AF.Sqrt),
                       after=(f"{t}_var",), name=f"{t}_sqrt")
                sch.op("vector", lambda v: v.reciprocal(colD[:], colC[:]),
                       after=(f"{t}_sqrt",), name=f"{t}_rstd")
                accsl = acc[:, i * D:(i + 1) * D]
                aft_ln1 = []
                if g > 0:
                    aft_ln1.append(f"g{g-1}o{i}_ln2")
                def ln1n(v, accsl=accsl):
                    v.tensor_scalar(accsl, r1[:], colA[:], None, op0=ALU.subtract)
                    return v.tensor_scalar(accsl, accsl, colD[:], None, op0=ALU.mult)
                sch.op("vector", ln1n, after=aft_ln1, name=f"{t}_ln1")
                if g == 0 and i == 0:
                    tapst("tap_ln1", acc[:, 0:D], (f"{t}_ln1",))

                def tln(eng, accsl=accsl):
                    for c2 in range(8):
                        inst = eng.transpose(ps[3][:, c2 * P:(c2 + 1) * P],
                                             accsl[:, c2 * P:(c2 + 1) * P], eye[:])
                    return inst
                sch.op("tensor", tln, after=(f"{t}_ln1", f"{t}_eattnfm"), name=f"{t}_tln")
                lnout = ln1fm[:].rearrange("p (dc t) -> p dc t", dc=8)[:, :, i * P:(i + 1) * P]
                aft_elnfm = [f"{t}_tln"]
                if i == 0 and g > 0:
                    aft_elnfm.append(f"g{g-1}_ffn_mm_done")
                sch.op("scalar", lambda s, lnout=lnout: s.copy(
                    lnout, ps[3][:].rearrange("p (dc t) -> p dc t", dc=8, t=P)),
                    after=tuple(aft_elnfm), name=f"{t}_elnfm")

                def gate(eng, i=i):
                    for kc in range(8):
                        inst = eng.matmul(
                            pgate,
                            ln1fm[:, kc * TG + i * P: kc * TG + (i + 1) * P],
                            gwT[:, kc * E:(kc + 1) * E],
                            start=(kc == 0), stop=(kc == 7))
                    return inst
                aft_gate = [f"{t}_elnfm", "c_gw"]
                if i:
                    aft_gate.append(f"{pg}t{i-1}_cw")
                elif g:
                    aft_gate.append(f"g{g-1}t{NT-1}_cw")
                sch.op("tensor", gate, after=aft_gate, name=f"{t}_gate")
                sch.op("scalar", lambda s: s.copy(gsc[:], pgate),
                       after=(f"{t}_gate",), name=f"{t}_egate")
                sch.op("scalar", lambda s: s.activation(gtmp[:], gsc[:], AF.Exp),
                       after=(f"{t}_egate",), name=f"{t}_gexp")
                cwsl = cw[:, i * E:(i + 1) * E]
                def top2(v, cwsl=cwsl):
                    colAb = colA[:].broadcast_to([P, E])
                    v.tensor_reduce(colB[:], gtmp[:], axis=AX.X, op=ALU.add)
                    v.drain()
                    v.reciprocal(colC[:], colB[:])
                    v.drain()
                    v.tensor_scalar(gsc[:], gtmp[:], colC[:], None, op0=ALU.mult)
                    v.drain()
                    v.tensor_tensor(gtmp[:], gsc[:], gbr[:], op=ALU.add)
                    v.drain()
                    v.reduce_max(colA[:], gtmp[:], axis=AX.X)
                    v.drain()
                    v.tensor_tensor(gtmp2[:], gtmp[:], colAb, op=ALU.is_equal)
                    v.drain()
                    v.scalar_tensor_tensor(gtmp[:], gtmp2[:], NEGBIG, gtmp[:], op0=ALU.mult, op1=ALU.add)
                    v.drain()
                    v.reduce_max(colA[:], gtmp[:], axis=AX.X)
                    v.drain()
                    v.tensor_tensor(gtmp[:], gtmp[:], colAb, op=ALU.is_equal)
                    v.drain()
                    v.tensor_tensor(gtmp[:], gtmp[:], gtmp2[:], op=ALU.add)
                    v.drain()
                    return v.tensor_tensor(cwsl, gsc[:], gtmp[:], op=ALU.mult)
                sch.op("vector", top2, after=(f"{t}_gexp", "c_gb"), name=f"{t}_cw")
                if g == 0 and i == 0:
                    tapst("tap_gsc", gsc[:], (f"{t}_cw",))
                    tapst("tap_cw", cw[:, 0:E], (f"{t}_cw",))

                if g == 0 and i == NT - 1:
                    tapst("tap_ln1fm", ln1fm[:], (f"{t}_elnfm",))

            def w13_src(base, pair):
                return base.rearrange("(kc p) f -> p kc f", kc=8, p=P)[:, :, pair * 256:(pair + 1) * 256]

            NTS = TG // TS
            for e_ in range(E + 1):
                x_ = f"{pg}e{e_}"
                w1b = w1_in[e_] if e_ < E else sw1_in
                w3b = w3_in[e_] if e_ < E else sw3_in
                w2b = (w2_in[e_] if e_ < E else sw2_in).rearrange("(fc p) d2 -> p fc d2", fc=8, p=P)

                aft = [f"{pg}_w2mm_{e_-1}_{NT-1}"] if e_ else [f"{pg}t{NT-1}_qkv"]
                ld("gpsimd", r3(w2s), w2b, f"{x_}_lw2", after=aft)
                ld("gpsimd", b2row[:], b2s_in[e_:e_ + 1, :], f"{x_}_lb2", after=aft)
                sch.op("vector", lambda v: v.tensor_scalar_mul(w2s, w2s, INV_ALPHA),
                       after=(f"{x_}_lw2",), name=f"{x_}_w2sc")

                if e_ < E:
                    b1o, b3o = 16 * e_, 128 + 16 * e_
                else:
                    b1o, b3o = 256, 272

                for m in range(4):
                    def hq(ee, mm):
                        return [f"{pg}e{ee}_h1g_{2*mm+1}", f"{pg}e{ee}_h3g_{2*mm+1}",
                                f"{pg}e{ee}_h1l_{2*mm+1}", f"{pg}e{ee}_h3l_{2*mm+1}"]
                    if m > 0:
                        aft_l = hq(e_, m - 1)
                    elif e_ > 0:
                        aft_l = hq(e_ - 1, 3)
                    else:
                        aft_l = [f"{pg}t{NT-1}_qkv"]
                    def w13_both(base, mm):
                        return base.rearrange("(kc p) (r q n) -> p q r kc n",
                                              kc=8, p=P, r=2, q=4)[:, mm]
                    w1gl = wbig[:, 0:4096].rearrange("p (r kc n) -> p r kc n", r=2, kc=8)
                    w3gl = wbig[:, 4096:8192].rearrange("p (r kc n) -> p r kc n", r=2, kc=8)
                    ld("gpsimd", w1gl, w13_both(w1b, m), f"{x_}_lw1g_{m}", after=aft_l)
                    sch.alias(f"{x_}_lw1l_{m}", f"{x_}_lw1g_{m}")
                    ld("gpsimd", w3gl, w13_both(w3b, m), f"{x_}_lw3g_{m}", after=aft_l)
                    sch.alias(f"{x_}_lw3l_{m}", f"{x_}_lw3g_{m}")

                    for j2 in range(2):
                        p_ = 2 * m + j2
                        fcg = p_
                        fcl = 8 + p_
                        aft_g = [f"{x_}_lw1g_{m}", f"{x_}_lw3g_{m}", f"{pg}t{NT-1}_elnfm"]
                        if p_ > 0:
                            aft_g.append(f"{x_}_ug_{p_-1}")
                        elif e_ > 0:
                            aft_g.append(f"{pg}_acc_{e_-1}_{NT-1}")
                        else:
                            aft_g += [f"{pg}t{NT-1}_r1", f"{pg}t{NT-1}_eqkv",
                                      f"{pg}t{NT-1}_attnb"]
                        def h1gm(eng, j2=j2):
                            for ts in range(NTS):
                                for kc in range(8):
                                    inst = eng.matmul(
                                        ps[0][:, ts * TS:(ts + 1) * TS],
                                        w1g[:, kc * 256 + j2 * 128: kc * 256 + (j2 + 1) * 128],
                                        ln1fm[:, kc * TG + ts * TS: kc * TG + (ts + 1) * TS],
                                        start=(kc == 0), stop=(kc == 7))
                            return inst
                        def h3gm(eng, j2=j2):
                            for ts in range(NTS):
                                for kc in range(8):
                                    inst = eng.matmul(
                                        ps[1][:, ts * TS:(ts + 1) * TS],
                                        w3g[:, kc * 256 + j2 * 128: kc * 256 + (j2 + 1) * 128],
                                        ln1fm[:, kc * TG + ts * TS: kc * TG + (ts + 1) * TS],
                                        start=(kc == 0), stop=(kc == 7))
                            return inst
                        sch.op("tensor", h1gm, after=aft_g, name=f"{x_}_h1g_{p_}")
                        sch.op("tensor", h3gm, after=aft_g, name=f"{x_}_h3g_{p_}")
                        aft_t3 = [f"{x_}_h3g_{p_}", "c_bcol"]
                        if p_ > 0:
                            aft_t3.append(f"{x_}_ul_{p_-1}")
                        elif e_ > 0:
                            aft_t3.append(f"{pg}e{e_-1}_ul_7")
                        sch.op("scalar",
                               lambda s, b3_ap=bcol[:, b3o + fcg: b3o + fcg + 1]:
                               s.activation(t3b[:, :TG], ps[1][:, :TG], AF.Identity, bias=b3_ap),
                               after=aft_t3, name=f"{x_}_t3g_{p_}")
                        aft_u = [f"{x_}_h1g_{p_}", f"{x_}_t3g_{p_}"]
                        if p_ > 0:
                            aft_u.append(f"{x_}_sil_{p_-1}")
                        elif e_ > 0:
                            aft_u.append(f"{pg}e{e_-1}_sil_7")
                        def ugf(v, b1_ap=bcol[:, b1o + fcg: b1o + fcg + 1]):
                            for c0 in range(0, TG, 512):
                                c1_ = min(c0 + 512, TG)
                                inst = v.scalar_tensor_tensor(
                                    ug[:, c0:c1_], ps[0][:, c0:c1_], b1_ap, t3b[:, c0:c1_],
                                    op0=ALU.add, op1=ALU.mult)
                            return inst
                        sch.op("vector", ugf, after=aft_u, name=f"{x_}_ug_{p_}")
                        aft_l2 = [f"{x_}_lw1l_{m}", f"{x_}_lw3l_{m}", f"{pg}t{NT-1}_elnfm"]
                        if p_ > 0:
                            aft_l2.append(f"{x_}_ul_{p_-1}")
                        elif e_ > 0:
                            aft_l2.append(f"{pg}e{e_-1}_ul_7")
                        else:
                            aft_l2 += [f"{pg}t{NT-1}_attnb", f"{pg}t{NT-1}_egate"]
                        def h1lm(eng, j2=j2):
                            for ts in range(NTS):
                                for kc in range(8):
                                    inst = eng.matmul(
                                        ps[2][:, ts * TS:(ts + 1) * TS],
                                        w1l[:, kc * 256 + j2 * 128: kc * 256 + (j2 + 1) * 128],
                                        ln1fm[:, kc * TG + ts * TS: kc * TG + (ts + 1) * TS],
                                        start=(kc == 0), stop=(kc == 7))
                            return inst
                        def h3lm(eng, j2=j2):
                            for ts in range(NTS):
                                for kc in range(8):
                                    inst = eng.matmul(
                                        ps[3][:, ts * TS:(ts + 1) * TS],
                                        w3l[:, kc * 256 + j2 * 128: kc * 256 + (j2 + 1) * 128],
                                        ln1fm[:, kc * TG + ts * TS: kc * TG + (ts + 1) * TS],
                                        start=(kc == 0), stop=(kc == 7))
                            return inst
                        sch.op("tensor", h1lm, after=aft_l2, name=f"{x_}_h1l_{p_}")
                        sch.op("tensor", h3lm, after=aft_l2, name=f"{x_}_h3l_{p_}")
                        sch.op("scalar",
                               lambda s, b3_ap=bcol[:, b3o + fcl: b3o + fcl + 1]:
                               s.activation(t3b[:, :TG], ps[3][:, :TG], AF.Identity, bias=b3_ap),
                               after=(f"{x_}_h3l_{p_}", f"{x_}_ug_{p_}"), name=f"{x_}_t3l_{p_}")
                        def ulf(v, b1_ap=bcol[:, b1o + fcl: b1o + fcl + 1]):
                            for c0 in range(0, TG, 512):
                                c1_ = min(c0 + 512, TG)
                                inst = v.scalar_tensor_tensor(
                                    ul[:, c0:c1_], ps[2][:, c0:c1_], b1_ap, t3b[:, c0:c1_],
                                    op0=ALU.add, op1=ALU.mult)
                            return inst
                        sch.op("vector", ulf, after=(f"{x_}_h1l_{p_}", f"{x_}_t3l_{p_}"),
                               name=f"{x_}_ul_{p_}")
                        sch.op("vector",
                               lambda v: v.tensor_scalar(ug[:, :TG], ug[:, :TG], LIMIT, None, op0=ALU.min),
                               name=f"{x_}_gmin_{p_}")
                        sch.op("scalar", lambda s: s.activation(silu[:, :TG], ug[:, :TG], AF.Silu, scale=ALPHA),
                               after=(f"{x_}_gmin_{p_}",), name=f"{x_}_sil_{p_}")
                        sch.op("vector",
                               lambda v: v.tensor_scalar(ul[:, :TG], ul[:, :TG], -LIMIT, LIMIT,
                                                         op0=ALU.max, op1=ALU.min),
                               name=f"{x_}_lc_{p_}")
                        if g == 0 and e_ == 1 and p_ == 0:
                            _tp = tapst("tap_afm", afm[:], (f"{pg}e0_a_7",))
                            if _tp:
                                aft_a_extra = (_tp,)
                        atgt = afm[:, fcg * TG:(fcg + 1) * TG]
                        aft_a = [f"{x_}_sil_{p_}"]
                        if e_ > 0:
                            aft_a.append(f"{pg}_w2mm_{e_-1}_{NT-1}")
                        if taps and g == 0 and e_ == 1 and p_ == 0 and "tap_afm" in sch.byname:
                            aft_a.append("tap_afm")
                        elif g > 0:
                            aft_a.append(f"g{g-1}_w2mm_{E}_{NT-1}")
                        sch.op("vector",
                               lambda v, atgt=atgt: v.scalar_tensor_tensor(
                                   atgt, ul[:, :TG], 1.0, silu[:, :TG], op0=ALU.add, op1=ALU.mult),
                               after=aft_a, name=f"{x_}_a_{p_}")


                for i in range(NT):
                    yp = ps[0] if i % 2 == 0 else ps[1]
                    def w2mm(eng, yp=yp, i=i):
                        for ds in range(2):
                            eng.matmul(yp[:, ds * 512:(ds + 1) * 512], ones_r[0:1, :],
                                       b2row[0:1, ds * 512:(ds + 1) * 512], start=True, stop=False)
                            for fc in range(8):
                                inst = eng.matmul(
                                    yp[:, ds * 512:(ds + 1) * 512],
                                    afm[:, fc * TG + i * P: fc * TG + (i + 1) * P],
                                    w2s[:, fc * D + ds * 512: fc * D + (ds + 1) * 512],
                                    start=False, stop=(fc == 7))
                        return inst
                    aft = [f"{x_}_a_7", f"{x_}_w2sc", f"{x_}_lb2"]
                    if i >= 2:
                        aft.append(f"{x_}_acc_{i-2}")
                    elif e_ == 0:
                        aft += [f"{x_}_ug_7", f"{x_}_t3g_7"]
                    else:
                        aft.append(f"{pg}_acc_{e_-1}_{NT-2+i}")
                    sch.op("tensor", w2mm, after=aft, name=f"{x_}_w2mm_{i}")
                    if i == NT - 1:
                        sch.alias(f"{pg}_w2mm_{e_}_{NT-1}", f"{x_}_w2mm_{i}")
                    accsl = acc[:, i * D:(i + 1) * D]
                    if e_ < E:
                        cwcol = cw[:, i * E + e_: i * E + e_ + 1]
                        def accf(v, yp=yp, accsl=accsl, cwcol=cwcol):
                            for c0 in range(0, D, 512):
                                v.scalar_tensor_tensor(
                                    silu[:, c0:c0 + 512], yp[:, c0:c0 + 512], cwcol,
                                    accsl[:, c0:c0 + 512], op0=ALU.mult, op1=ALU.add)
                            return v.tensor_copy(accsl, silu[:, :D])
                    else:
                        def accf(v, yp=yp, accsl=accsl):
                            return v.tensor_tensor(accsl, accsl, yp[:, :D], op=ALU.add)
                    sch.op("vector", accf, after=(f"{x_}_w2mm_{i}",), name=f"{pg}_acc_{e_}_{i}")
                    sch.alias(f"{x_}_acc_{i}", f"{pg}_acc_{e_}_{i}")
                    if g == 0 and e_ == E and i == 0:
                        tapst("tap_acc", acc[:, 0:D], (f"{pg}_acc_{e_}_{i}",))

            sch.alias(f"{pg}_ffn_mm_done", f"{pg}_w2mm_{E}_{NT-1}")
            sch.alias(f"{pg}_ffn_done", f"{pg}_acc_{E}_{NT-1}")

            for i in range(NT):
                t = f"{pg}o{i}"
                accsl = acc[:, i * D:(i + 1) * D]
                sch.op("vector", lambda v, accsl=accsl: v.tensor_reduce(colA[:], accsl, axis=AX.X, op=ALU.add),
                       after=(f"{pg}_acc_{E}_{i}",), name=f"{t}_s1")
                sch.op("scalar", lambda s, accsl=accsl: s.activation(sq[:], accsl, AF.Square),
                       after=(f"{pg}_acc_{E}_{i}",), name=f"{t}_sq")
                def ln_small2(v):
                    v.tensor_reduce(colB[:], sq[:], axis=AX.X, op=ALU.add)
                    v.drain()
                    v.tensor_scalar_mul(colA[:], colA[:], 1.0 / D)
                    v.tensor_scalar_mul(colB[:], colB[:], 1.0 / D)
                    v.drain()
                    v.tensor_tensor(colC[:], colA[:], colA[:], op=ALU.mult)
                    v.drain()
                    v.tensor_tensor(colB[:], colB[:], colC[:], op=ALU.subtract)
                    v.drain()
                    return v.tensor_scalar_add(colB[:], colB[:], float(EPS))
                sch.op("vector", ln_small2, after=(f"{t}_sq",), name=f"{t}_var")
                sch.op("scalar", lambda s: s.activation(colC[:], colB[:], AF.Sqrt),
                       after=(f"{t}_var",), name=f"{t}_sqrt")
                sch.op("vector", lambda v: v.reciprocal(colD[:], colC[:]),
                       after=(f"{t}_sqrt",), name=f"{t}_rstd")
                aft = [f"{t}_rstd"]
                if i >= 1:
                    aft.append(f"{pg}o{i-1}_st")
                elif g > 0:
                    aft.append(f"g{g-1}o{NT-1}_st")
                def ln2n(v, accsl=accsl):
                    v.tensor_scalar(outb[:], accsl, colA[:], None, op0=ALU.subtract)
                    return v.tensor_scalar(outb[:], outb[:], colD[:], None, op0=ALU.mult)
                sch.op("vector", ln2n, after=aft, name=f"{t}_ln2")
                rows = slice((g * NT + i) * P, (g * NT + i + 1) * P)
                sch.op("sync", lambda s, rows=rows: s.dma_start(out=out_t[rows, :], in_=outb[:]),
                       after=(f"{t}_ln2",), name=f"{t}_st", dma=True)

        for g in range(NG):
            emit_group(g)

        sch.op("sync", lambda s: s.nop(),
               after=tuple(f"g{NG-1}o{i}_st" for i in range(NT)), name="fin")

        if max_ops is not None:
            sch.ops = sch.ops[:max_ops]
        sch.build(block, sems)
    return nc


_NC_CACHE = {}


def _get_nc(TPC=4096, TG=1024):
    key = (TPC, TG)
    if key not in _NC_CACHE:
        _NC_CACHE[key] = build_nc(TPC, TG)
    return _NC_CACHE[key]


def prep_weights(kw):
    bf = ml_dtypes.bfloat16
    f32 = np.float32
    d = {}
    d["wq_in"] = np.ascontiguousarray(np.asarray(kw["L0_attn_Wq"], f32))
    d["wk_in"] = np.ascontiguousarray(np.asarray(kw["L0_attn_Wk"], f32))
    d["wv_in"] = np.ascontiguousarray(np.asarray(kw["L0_attn_Wv"], f32))
    d["wo_in"] = np.ascontiguousarray(np.asarray(kw["L0_attn_Wo"], f32))

    def permute_w13(w):
        wt = np.swapaxes(np.asarray(w, f32), -1, -2)
        return np.ascontiguousarray(np.concatenate([wt[..., 0::2], wt[..., 1::2]], axis=-1))

    d["w1_in"] = permute_w13(kw["L0_ffn_w1"])
    d["w3_in"] = permute_w13(kw["L0_ffn_w3"])
    d["sw1_in"] = permute_w13(kw["L0_ffn_sw1"])
    d["sw3_in"] = permute_w13(kw["L0_ffn_sw3"])
    d["w2_in"] = np.ascontiguousarray(np.swapaxes(np.asarray(kw["L0_ffn_w2"], f32), -1, -2))
    d["sw2_in"] = np.ascontiguousarray(np.asarray(kw["L0_ffn_sw2"], f32).T)

    brow = np.concatenate([
        np.asarray(kw["L0_attn_bq"], f32), np.asarray(kw["L0_attn_bk"], f32),
        np.asarray(kw["L0_attn_bv"], f32), np.asarray(kw["L0_attn_bo"], f32)])
    d["brow_in"] = np.ascontiguousarray(brow.reshape(1, 4 * 1024))
    d["b2s_in"] = np.ascontiguousarray(np.concatenate(
        [np.asarray(kw["L0_ffn_b2"], f32), np.asarray(kw["L0_ffn_sb2"], f32).reshape(1, -1)], axis=0))

    def colify(b):
        b = np.asarray(b, f32)
        bp = np.concatenate([b[0::2], b[1::2]])
        return np.ascontiguousarray(bp.reshape(16, 128).T)

    b1 = np.asarray(kw["L0_ffn_b1"], f32)
    b3 = np.asarray(kw["L0_ffn_b3"], f32)
    cols = [colify(b1[e]) for e in range(E)] + [colify(b3[e]) for e in range(E)] \
        + [colify(kw["L0_ffn_sb1"]), colify(kw["L0_ffn_sb3"])]
    d["bcol_in"] = np.ascontiguousarray(np.concatenate(cols, axis=1))

    gwT = np.asarray(kw["L0_ffn_gate_w"], f32).T.reshape(8, 128, E).transpose(1, 0, 2)
    d["gwT_in"] = np.ascontiguousarray(gwT.reshape(128, 64))
    d["gb_in"] = np.ascontiguousarray(np.broadcast_to(np.asarray(kw["L0_ffn_gate_b"], f32), (P, E)))

    d["eye_in"] = np.eye(P, dtype=f32)
    m = np.tril(np.ones((QC, QC), dtype=np.float32))
    d["msk_in"] = np.ascontiguousarray(np.broadcast_to(m.reshape(1, QC * QC), (P, QC * QC)).astype(bf))
    return d


def kernel(**inputs):
    x = np.asarray(inputs["L0_attn_x"], dtype=np.float32).reshape(-1, D)
    n_cores = 8
    shards = np.split(np.ascontiguousarray(x), n_cores, axis=0)
    wmap = prep_weights(inputs)
    nc = _get_nc(shards[0].shape[0])
    in_maps = []
    for s in shards:
        m = dict(wmap)
        m["x_in"] = s
        in_maps.append(m)
    res = run_bass_kernel_spmd(nc, in_maps, core_ids=list(range(n_cores)))
    out = np.concatenate([np.asarray(r["out"]) for r in res.results], axis=0)
    return out.reshape(2, 16384, D)


if __name__ == "__main__":
    nc = build_nc()
    print("build ok")
